# version 08
# baseline (speedup 1.0000x reference)
from contextlib import ExitStack

import numpy as np
import ml_dtypes
import concourse.bass as bass
import concourse.mybir as mybir
from concourse.bass import IndirectOffsetOnAxis
from concourse.bass_utils import run_bass_kernel_spmd

F32 = mybir.dt.float32
BF16 = mybir.dt.bfloat16
U32 = mybir.dt.uint32
AF = mybir.ActivationFunctionType
ALU = mybir.AluOpType
AX = mybir.AxisListType

N_CORES = 8
B, S, HID = 16, 4096, 1024
H, D = 16, 64
E, I = 8, 4096
B_CORE = B // N_CORES
T_LOC = B_CORE * S
N_TILES = T_LOC // 128
N_CHUNK = S // 128
C_LOC = 2304
ST = 512
ST_SIZES = (512, 512, 512, 512, 256)
N_ST = len(ST_SIZES)
KH = HID // 128
MI = I // 128
RMS_EPS = 1e-5
L2_EPS = 1e-24
LN_EPS = 1e-5


def _consts():
    j = np.arange(128)[:, None].astype(np.float32)
    i = np.arange(128)[None, :].astype(np.float32)
    c = {}
    c["ident"] = np.eye(128, dtype=ml_dtypes.bfloat16)
    c["identf"] = np.eye(128, dtype=np.float32)
    c["onescolf"] = np.ones((128, 1), dtype=np.float32)
    c["Mpos"] = np.maximum(i - 1.0 - j, 0.0).astype(np.float32)
    c["mask"] = (j < i).astype(np.float32)
    c["Ltri"] = (j < i).astype(ml_dtypes.bfloat16)
    c["onescol"] = np.ones((128, 1), dtype=ml_dtypes.bfloat16)
    c["ones1"] = np.ones((1, 128), dtype=np.float32)
    c["iotap"] = np.ascontiguousarray(j)
    c["iota127"] = np.ascontiguousarray(127.0 - j)
    c["iota8"] = np.tile(np.arange(8, dtype=np.float32)[None, :], (128, 1))
    c["iota8m"] = np.tile(np.arange(8, dtype=np.float32)[None, :] - 99.0, (128, 1))
    c["expbase"] = np.tile((np.arange(8, dtype=np.float32) * C_LOC)[None, :], (128, 1))
    return c


class _Sems:
    def __init__(self):
        self.h = {}
        self.n = {}

    def add(self, name, handle):
        self.h[name] = handle
        self.n[name] = 0

    def snap(self):
        return dict(self.n)

    def restore(self, s):
        self.n = dict(s)

    def inc(self, instr, name, k):
        self.n[name] += k
        if instr is not None:
            instr.then_inc(self.h[name], k)
        return self.n[name]

    def val(self, name):
        return self.n[name]


SEM_NAMES = ("dma_x", "dma_w", "dma_s", "dma_g", "dma_v", "dma_t", "te", "ve", "ac", "pl", "st")


def build_program(phases=("ffn1", "attn", "ffn2"), tap=None):
    nc = bass.Bass("TRN2", target_bir_lowering=False, debug=False)

    xin = nc.dram_tensor("x_sh", [T_LOC, HID], F32, kind="ExternalInput").ap()
    wq = nc.dram_tensor("Wq", [HID, HID], F32, kind="ExternalInput").ap()
    wk = nc.dram_tensor("Wk", [HID, HID], F32, kind="ExternalInput").ap()
    wv = nc.dram_tensor("Wv", [HID, HID], F32, kind="ExternalInput").ap()
    wf = nc.dram_tensor("Wf", [HID, HID], F32, kind="ExternalInput").ap()
    wb_in = nc.dram_tensor("Wb", [HID, H], F32, kind="ExternalInput").ap()
    alog = nc.dram_tensor("A_log", [H], F32, kind="ExternalInput").ap()
    dtb = nc.dram_tensor("dt_bias", [HID], F32, kind="ExternalInput").ap()
    wg = nc.dram_tensor("Wg", [HID, HID], F32, kind="ExternalInput").ap()
    bgv = nc.dram_tensor("bg", [HID], F32, kind="ExternalInput").ap()
    wo = nc.dram_tensor("Wo", [HID, HID], F32, kind="ExternalInput").ap()
    nw = nc.dram_tensor("norm_w", [D], F32, kind="ExternalInput").ap()
    rgate = nc.dram_tensor("router_gate", [HID, E], F32, kind="ExternalInput").ap()
    gate_w = nc.dram_tensor("gate_w", [E, HID, I], F32, kind="ExternalInput").ap()
    up_w = nc.dram_tensor("up_w", [E, HID, I], F32, kind="ExternalInput").ap()
    down_w = nc.dram_tensor("down_w", [E, I, HID], F32, kind="ExternalInput").ap()
    yout = nc.dram_tensor("y_sh", [T_LOC, HID], F32, kind="ExternalOutput").ap()
    if tap == "dbg1":
        dbg_w = nc.dram_tensor("dbg_w", [128, 4 * N_TILES], F32, kind="ExternalOutput").ap()
        dbg_xe = nc.dram_tensor("dbg_xe", [E * C_LOC, HID], BF16, kind="ExternalOutput").ap()
        dbg_ye = nc.dram_tensor("dbg_ye", [E * C_LOC, HID], BF16, kind="ExternalOutput").ap()
        dbg_ht = nc.dram_tensor("dbg_ht", [128, MI * ST], BF16, kind="ExternalOutput").ap()
        dbg_xet = nc.dram_tensor("dbg_xet", [128, KH * ST], BF16, kind="ExternalOutput").ap()
        dbg_ps = nc.dram_tensor("dbg_ps", [128, 3072], F32, kind="ExternalOutput").ap()
        dbg_dwn = nc.dram_tensor("dbg_dwn", [I, HID], BF16, kind="ExternalOutput").ap()
    DBG_EX, DBG_ST = 0, 2

    gate_bf = nc.dram_tensor("gate_bf", [E, HID, I], BF16, kind="Internal").ap()
    up_bf = nc.dram_tensor("up_bf", [E, HID, I], BF16, kind="Internal").ap()
    down_bf = nc.dram_tensor("down_bf", [E, I, HID], BF16, kind="Internal").ap()
    xe_d = nc.dram_tensor("XE", [E * C_LOC, HID], BF16, kind="Internal").ap()
    ye_d = nc.dram_tensor("YE", [E * C_LOC, HID], BF16, kind="Internal").ap()
    x1_d = nc.dram_tensor("x1_d", [T_LOC, HID], F32, kind="Internal").ap()
    x2_d = nc.dram_tensor("x2_d", [T_LOC, HID], F32, kind="Internal").ap()
    qT_d = nc.dram_tensor("qT_d", [N_TILES, KH, 128, 128], BF16, kind="Internal").ap()
    qsT_d = nc.dram_tensor("qsT_d", [N_TILES, KH, 128, 128], BF16, kind="Internal").ap()
    kT_d = nc.dram_tensor("kT_d", [N_TILES, KH, 128, 128], BF16, kind="Internal").ap()
    kk_d = nc.dram_tensor("kk_d", [N_TILES, 128, HID], BF16, kind="Internal").ap()
    vv_d = nc.dram_tensor("vv_d", [N_TILES, 128, HID], BF16, kind="Internal").ap()
    gg_d = nc.dram_tensor("gg_d", [N_TILES, 128, HID], BF16, kind="Internal").ap()
    gg2_d = nc.dram_tensor("gg2_d", [N_TILES, 128, HID], BF16, kind="Internal").ap()
    aux_d = nc.dram_tensor("aux_d", [N_TILES, 128, 64], F32, kind="Internal").ap()

    cdat = _consts()
    cd = {}
    for k, v in cdat.items():
        if v.dtype == ml_dtypes.bfloat16:
            cd[k] = nc.inline_tensor(np.ascontiguousarray(v.view(np.uint16)), f"c_{k}")
        else:
            cd[k] = nc.inline_tensor(np.ascontiguousarray(v), f"c_{k}")

    sems = _Sems()

    def run_phase(blk, emitfn):
        snap = sems.snap()
        for name, dec in (("sync", blk.sync), ("tensor", blk.tensor),
                          ("vector", blk.vector), ("scalar", blk.scalar),
                          ("gpsimd", blk.gpsimd)):
            sems.restore(snap)
            dec(lambda e, _n=name: emitfn(_n, e))

    class EC:

        def __init__(self, eng, e):
            self.eng = eng
            self.e = e
            self.SY = eng == "sync"
            self.TE = eng == "tensor"
            self.VE = eng == "vector"
            self.AC = eng == "scalar"
            self.GP = eng == "gpsimd"

        def barrier(self):
            for s in SEM_NAMES:
                if sems.val(s) > 0:
                    self.e.wait_ge(sems.h[s], sems.val(s))

        def v(self, f):
            if self.VE:
                self.e.wait_ge(sems.h["ve"], sems.val("ve"))
                i = f()
            else:
                i = None
            return sems.inc(i, "ve", 1)

        def a(self, f):
            if self.AC:
                self.e.wait_ge(sems.h["ac"], sems.val("ac"))
                i = f()
            else:
                i = None
            return sems.inc(i, "ac", 1)

        def p(self, f):
            if self.GP:
                self.e.wait_ge(sems.h["pl"], sems.val("pl"))
                i = f()
            else:
                i = None
            return sems.inc(i, "pl", 1)

        def t(self, f, inc=False):
            i = f() if self.TE else None
            if inc:
                return sems.inc(i, "te", 1)
            sems.n["te"] += 0
            return None

        def tmark(self, f):
            return self.t(f, inc=True)

        def dma(self, active, f, sem):
            i = f() if active else None
            return sems.inc(i, sem, 16)

        def w(self, name, val=None):
            self.e.wait_ge(sems.h[name], sems.val(name) if val is None else val)

    with ExitStack() as top:
        ent = top.enter_context
        for s in SEM_NAMES:
            sems.add(s, ent(nc.semaphore(s)))

        ident_sb = ent(nc.sbuf_tensor("ident_sb", [128, 128], BF16))
        identf_sb = ent(nc.sbuf_tensor("identf_sb", [128, 128], F32))
        onescolf_sb = ent(nc.sbuf_tensor("onescolf_sb", [128, 1], F32))
        cs_bc = ent(nc.sbuf_tensor("cs_bc", [128, 8], F32))
        ltri_sb = ent(nc.sbuf_tensor("ltri_sb", [128, 128], BF16))
        onescol_sb = ent(nc.sbuf_tensor("onescol_sb", [128, 1], BF16))
        ones1_sb = ent(nc.sbuf_tensor("ones1_sb", [1, 128], F32))
        iota8_sb = ent(nc.sbuf_tensor("iota8_sb", [128, 8], F32))
        iota8m_sb = ent(nc.sbuf_tensor("iota8m_sb", [128, 8], F32))
        expbase_sb = ent(nc.sbuf_tensor("expbase_sb", [128, 8], F32))
        iotap_sb = ent(nc.sbuf_tensor("iotap_sb", [128, 1], F32))
        iota127_sb = ent(nc.sbuf_tensor("iota127_sb", [128, 1], F32))
        dt_all = ent(nc.sbuf_tensor("dt_all", [128, H * 128], BF16))
        a_all = ent(nc.sbuf_tensor("a_all", [128, H], F32))
        kdec_all = ent(nc.sbuf_tensor("kdec_all", [128, H], F32))
        a128_all = ent(nc.sbuf_tensor("a128_all", [128, H], F32))
        lam_bc = ent(nc.sbuf_tensor("lam_bc", [128, H], F32))
        rg_sb = ent(nc.sbuf_tensor("rg_sb", [128, KH * 8], F32))
        nwrow_sb = ent(nc.sbuf_tensor("nwrow_sb", [1, HID], F32))
        m2_all = ent(nc.sbuf_tensor("m2_all", [128, N_TILES], F32))
        r2_all = ent(nc.sbuf_tensor("r2_all", [128, N_TILES], F32))
        s1_all = ent(nc.sbuf_tensor("s1_all", [128, N_TILES], F32))
        s2_all = ent(nc.sbuf_tensor("s2_all", [128, N_TILES], F32))
        w1s = ent(nc.sbuf_tensor("w1s", [128, N_TILES], F32))
        w2s = ent(nc.sbuf_tensor("w2s", [128, N_TILES], F32))
        d1u = ent(nc.sbuf_tensor("d1u", [128, N_TILES], U32))
        d2u = ent(nc.sbuf_tensor("d2u", [128, N_TILES], U32))
        base_sb = ent(nc.sbuf_tensor("base_sb", [1, 8], F32))
        eps5_sb = ent(nc.sbuf_tensor("eps5_sb", [128, 1], F32))
        eps24_sb = ent(nc.sbuf_tensor("eps24_sb", [128, 1], F32))

        with ExitStack() as isc:
            en = isc.enter_context
            mpos_c = en(nc.sbuf_tensor("mpos_c", [128, 128], F32))
            mask_c = en(nc.sbuf_tensor("mask_c", [128, 128], F32))
            alog_sb = en(nc.sbuf_tensor("alog_sb", [1, H], F32))
            lam1_sb = en(nc.sbuf_tensor("lam1_sb", [1, H], F32))
            t16a = en(nc.sbuf_tensor("t16a", [128, H], F32))
            t16b = en(nc.sbuf_tensor("t16b", [128, H], F32))
            t128a = en(nc.sbuf_tensor("t128a", [128, 128], F32))
            t128b = en(nc.sbuf_tensor("t128b", [128, 128], F32))
            ps_i = en(nc.psum_tensor("ps_init", [128, 512], F32))

            def emit_init(eng, e):
                c = EC(eng, e)
                for nm, t in (("ident", ident_sb), ("Ltri", ltri_sb), ("onescol", onescol_sb)):
                    c.dma(c.SY, lambda nm=nm, t=t: e.dma_start(
                        out=t.bitcast(mybir.dt.uint16)[:], in_=cd[nm].ap()), "dma_x")
                for nm, t in (("identf", identf_sb), ("onescolf", onescolf_sb),
                              ("ones1", ones1_sb), ("iota8", iota8_sb), ("iota8m", iota8m_sb),
                              ("expbase", expbase_sb), ("Mpos", mpos_c), ("mask", mask_c),
                              ("iotap", iotap_sb), ("iota127", iota127_sb)):
                    c.dma(c.SY, lambda nm=nm, t=t: e.dma_start(out=t[:], in_=cd[nm].ap()), "dma_x")
                c.dma(c.SY, lambda: e.dma_start(out=alog_sb[0:1, :], in_=alog[None, :]), "dma_x")
                for t in range(H):
                    c.dma(c.SY, lambda t=t: e.dma_start(
                        out=nwrow_sb[0:1, t * D:(t + 1) * D], in_=nw[None, :]), "dma_x")
                rgl = c.dma(c.SY, lambda: e.dma_start(
                    out=rg_sb[:].rearrange("p (k e) -> p k e", e=8),
                    in_=rgate.rearrange("(k p) e -> p k e", p=128)), "dma_x")
                ld = sems.val("dma_x")
                if c.AC:
                    c.w("dma_x", ld)
                c.a(lambda: nc.scalar.activation(lam1_sb[0:1, :], alog_sb[0:1, :], AF.Exp))
                aa = c.a(lambda: nc.scalar.mul(lam1_sb[0:1, :], lam1_sb[0:1, :], -1.0))
                if c.TE:
                    c.w("ac", aa)
                    c.w("dma_x", ld)
                tm = c.tmark(lambda: nc.tensor.matmul(ps_i[:, 0:16], ones1_sb[:],
                                                      lam1_sb[0:1, :], start=True, stop=True))
                if c.VE:
                    c.w("te", tm)
                c.v(lambda: nc.vector.tensor_copy(lam_bc[:], ps_i[:, 0:16]))
                c.v(lambda: nc.vector.tensor_tensor(
                    t16a[:], iotap_sb[:].broadcast_to([128, H]), lam_bc[:], ALU.mult))
                vv = c.v(lambda: nc.vector.tensor_tensor(
                    t16b[:], iota127_sb[:].broadcast_to([128, H]), lam_bc[:], ALU.mult))
                if c.AC:
                    c.w("ve", vv)
                c.a(lambda: nc.scalar.activation(a_all[:], t16a[:], AF.Exp))
                c.a(lambda: nc.scalar.activation(kdec_all[:], t16b[:], AF.Exp))
                c.a(lambda: nc.scalar.activation(a128_all[:], lam_bc[:], AF.Exp, scale=128.0))
                for h in range(H):
                    vv = c.v(lambda h=h: nc.vector.tensor_scalar(
                        t128a[:], mpos_c[:], lam_bc[:, h:h + 1], None, ALU.mult))
                    if c.AC:
                        c.w("ve", vv)
                    av = c.a(lambda: nc.scalar.activation(t128b[:], t128a[:], AF.Exp))
                    if c.VE:
                        c.w("ac", av)
                    c.v(lambda h=h: nc.vector.tensor_tensor(
                        dt_all[:, h * 128:(h + 1) * 128], t128b[:], mask_c[:], ALU.mult))
                c.v(lambda: nc.vector.memset(base_sb[0:1, :], 0.0))
                c.v(lambda: nc.vector.memset(eps5_sb[:], LN_EPS))
                c.v(lambda: nc.vector.memset(eps24_sb[:], L2_EPS))
                if c.TE:
                    c.w("dma_x", ld)
                    c.w("ve")
                for k in range(KH):
                    cmm = c.tmark(lambda k=k: nc.tensor.matmul(
                        ps_i[0:1, 32:40], onescolf_sb[:], rg_sb[:, k * 8:(k + 1) * 8],
                        start=(k == 0), stop=(k == KH - 1)))
                vv2 = c.v(lambda: nc.vector.tensor_copy(lam1_sb[0:1, 0:8], ps_i[0:1, 32:40])) if False else None
                if c.VE:
                    c.w("te", cmm)
                c.v(lambda: nc.vector.tensor_copy(alog_sb[0:1, 0:8], ps_i[0:1, 32:40]))
                cbv = sems.val("ve")
                if c.TE:
                    c.w("ve", cbv)
                bmm = c.tmark(lambda: nc.tensor.matmul(
                    ps_i[:, 64:72], ones1_sb[:], alog_sb[0:1, 0:8], start=True, stop=True))
                if c.VE:
                    c.w("te", bmm)
                c.v(lambda: nc.vector.tensor_copy(cs_bc[:], ps_i[:, 64:72]))

            with nc.Block("init") as blk:
                run_phase(blk, emit_init)

        def ffn_sublayer(tag, x_src, x_dst, stash_stats, first_call):
            with ExitStack() as fsc:
                fe = fsc.enter_context
                gate_res = fe(nc.sbuf_tensor(f"gres_{tag}", [128, KH * I], BF16))
                up_res = fe(nc.sbuf_tensor(f"ures_{tag}", [128, KH * I], BF16))
                cvt = [fe(nc.sbuf_tensor(f"cvt{i}_{tag}", [128, I // 4], BF16)) for i in range(2)]
                wconv = {}
                conv_all = []
                if first_call:
                    for _ex in range(E):
                        for _wsrc, _wdst, _nsl in ((gate_w, gate_bf, KH), (up_w, up_bf, KH),
                                                   (down_w, down_bf, MI)):
                            _cols = _wsrc.shape[2]
                            _nh = max(1, _cols // (I // 4))
                            for _k in range(_nsl):
                                for _hh in range(_nh):
                                    conv_all.append((_ex, _wsrc, _wdst, _k, _hh, _cols // _nh))

                def make_conv(c, e, sems, start_expert=0):
                    i0 = sum(1 for it_ in conv_all if it_[0] < start_expert)
                    state = {"i": i0, "loads": {}}

                    def emit_conv(n, upto_expert=None):
                        while state["i"] < len(conv_all):
                            if n <= 0:
                                return
                            i = state["i"]
                            ex, wsrc, wdst, k, hh, hc = conv_all[i]
                            if upto_expert is not None and ex > upto_expert:
                                return
                            pp = i % 2
                            if c.GP and (i - 2) in state["loads"]:
                                c.w("dma_g", state["loads"][i - 2][1])
                            lv = c.dma(c.GP, lambda ex=ex, wsrc=wsrc, k=k, pp=pp, hh=hh, hc=hc:
                                       e.dma_start(out=cvt[pp][:, 0:hc],
                                                   in_=wsrc[ex, k * 128:(k + 1) * 128,
                                                            hh * hc:(hh + 1) * hc]), "dma_g")
                            if c.GP:
                                c.w("dma_g", lv)
                            sv_ = c.dma(c.GP, lambda ex=ex, wdst=wdst, k=k, pp=pp, hh=hh, hc=hc:
                                        e.dma_start(out=wdst[ex, k * 128:(k + 1) * 128,
                                                             hh * hc:(hh + 1) * hc],
                                                    in_=cvt[pp][:, 0:hc]), "dma_g")
                            state["loads"][i] = (lv, sv_)
                            if i == len(conv_all) - 1 or conv_all[i + 1][0] != ex:
                                wconv[ex] = sv_
                            state["i"] += 1
                            n -= 1
                    return emit_conv

                with ExitStack() as asc:
                    an = asc.enter_context
                    xa = [an(nc.sbuf_tensor(f"xa{i}_{tag}", [128, HID], F32)) for i in range(2)]
                    xln = [an(nc.sbuf_tensor(f"xln{i}_{tag}", [128, HID], BF16)) for i in range(2)]
                    xaT = an(nc.sbuf_tensor(f"xaT_{tag}", [128, HID], F32))
                    sqt = an(nc.sbuf_tensor(f"sqt_{tag}", [128, HID], F32))
                    r8a = an(nc.sbuf_tensor(f"r8a_{tag}", [128, 8], F32))
                    r8b = an(nc.sbuf_tensor(f"r8b_{tag}", [128, 8], F32))
                    r8c = an(nc.sbuf_tensor(f"r8c_{tag}", [128, 8], F32))
                    oh1 = an(nc.sbuf_tensor(f"oh1_{tag}", [128, 8], F32))
                    oh2 = an(nc.sbuf_tensor(f"oh2_{tag}", [128, 8], F32))
                    ohb = an(nc.sbuf_tensor(f"ohb_{tag}", [128, 8], BF16))
                    slg = an(nc.sbuf_tensor(f"slg_{tag}", [128, 8], F32))
                    bvec = an(nc.sbuf_tensor(f"bvec_{tag}", [128, 8], F32))
                    cc = [an(nc.sbuf_tensor(f"cc{i}_{tag}", [128, 1], F32)) for i in range(8)]
                    psXT = an(nc.psum_tensor(f"psXT_{tag}", [128, 2048], F32))
                    psA = an(nc.psum_tensor(f"psA_{tag}", [128, 1024], F32))
                    ps8 = an(nc.psum_tensor(f"ps8_{tag}", [128, 1024], F32))

                    def emit_A(eng, e):
                        c = EC(eng, e)
                        c.barrier()
                        bc_reg = e.to_reg(E * C_LOC - 1) if c.GP else None
                        emit_conv = make_conv(c, e, sems)
                        scat_done = {}
                        evacT = {}
                        rmax_ve = {}
                        rexp_ac = {}
                        pdone = {}
                        for t in range(N_TILES):
                            par = t % 2
                            if c.SY and t >= 2:
                                c.w("dma_g", scat_done[t - 2])
                            ldv = c.dma(c.SY, lambda: e.dma_start(
                                out=xa[par][:], in_=x_src[t * 128:(t + 1) * 128, :]), "dma_x")
                            if c.VE:
                                c.w("dma_x", ldv)
                            c.v(lambda: nc.vector.reduce_sum(cc[0][:], xa[par][:], axis=AX.X))
                            c.v(lambda: nc.vector.tensor_tensor(sqt[:], xa[par][:], xa[par][:], ALU.mult))
                            c.v(lambda: nc.vector.reduce_sum(cc[1][:], sqt[:], axis=AX.X))
                            c.v(lambda: nc.vector.tensor_scalar(cc[2][:], cc[0][:], 1.0 / HID, None, ALU.mult))
                            c.v(lambda: nc.vector.tensor_tensor(cc[3][:], cc[2][:], cc[2][:], ALU.mult))
                            c.v(lambda: nc.vector.tensor_scalar(cc[4][:], cc[1][:], 1.0 / HID, None, ALU.mult))
                            vst = c.v(lambda: nc.vector.tensor_sub(cc[4][:], cc[4][:], cc[3][:]))
                            if c.AC:
                                c.w("ve", vst)
                            c.a(lambda: nc.scalar.activation(cc[5][:], cc[4][:], AF.Ln, bias=eps5_sb[:]))
                            acv = c.a(lambda: nc.scalar.activation(cc[5][:], cc[5][:], AF.Exp, scale=-0.5))
                            if c.VE:
                                c.w("ac", acv)
                                if t >= 2:
                                    c.w("dma_g", scat_done[t - 2])
                            lnv = c.v(lambda: nc.vector.tensor_scalar(
                                xln[par][:], xa[par][:], cc[2][:], cc[5][:], ALU.subtract, ALU.mult))
                            if c.TE:
                                c.w("dma_x", ldv)
                                if t >= 2:
                                    c.w("ve", evacT[t - 2])
                            for f in range(KH):
                                ti = c.t(lambda f=f: nc.tensor.transpose(
                                    psXT[:, par * 1024 + f * 128: par * 1024 + (f + 1) * 128],
                                    xa[par][:, f * 128:(f + 1) * 128], identf_sb[:]),
                                    inc=(f == KH - 1))
                            ti = sems.val("te")
                            if c.VE:
                                c.w("te", ti)
                            evacT[t] = c.v(lambda: nc.vector.tensor_copy(
                                xaT[:], psXT[:, par * 1024: par * 1024 + 1024]))
                            if c.TE:
                                c.w("ve", evacT[t])
                                if t >= 2:
                                    c.w("ve", rmax_ve[t - 2])
                                    c.w("ac", rexp_ac[t - 2])
                            for k in range(KH):
                                c.t(lambda k=k: nc.tensor.matmul(
                                    psA[:, par * 512: par * 512 + 8],
                                    xaT[:, k * 128:(k + 1) * 128], rg_sb[:, k * 8:(k + 1) * 8],
                                    start=(k == 0), stop=(k == KH - 1)), inc=(k == KH - 1))
                            rmm = sems.val("te")
                            rawlog = psA[:, par * 512: par * 512 + 8]
                            if c.VE:
                                c.w("te", rmm)
                            c.v(lambda: nc.vector.tensor_scalar(r8b[:], cs_bc[:], cc[2][:], None, ALU.mult))
                            c.v(lambda: nc.vector.tensor_tensor(r8a[:], rawlog, r8b[:], ALU.subtract))
                            c.v(lambda: nc.vector.tensor_scalar(r8a[:], r8a[:], cc[5][:], None, ALU.mult))
                            c.v(lambda: nc.vector.reduce_max(cc[6][:], r8a[:], axis=AX.X))
                            rmax_ve[t] = c.v(lambda: nc.vector.tensor_scalar(
                                cc[6][:], cc[6][:], -1.0, None, ALU.mult))
                            if c.AC:
                                c.w("ve", rmax_ve[t])
                            rexp_ac[t] = c.a(lambda: nc.scalar.activation(
                                r8a[:], r8a[:], AF.Exp, bias=cc[6][:]))
                            if c.VE:
                                c.w("ac", rexp_ac[t])
                            c.v(lambda: nc.vector.reduce_sum(cc[7][:], r8a[:], axis=AX.X))
                            c.v(lambda: nc.vector.reciprocal(cc[7][:], cc[7][:]))
                            c.v(lambda: nc.vector.tensor_scalar(r8a[:], r8a[:], cc[7][:], None, ALU.mult))
                            c.v(lambda: nc.vector.reduce_max(cc[0][:], r8a[:], axis=AX.X))
                            c.v(lambda: nc.vector.tensor_scalar(r8b[:], r8a[:], cc[0][:], None, ALU.subtract))
                            c.v(lambda: nc.vector.tensor_scalar(r8b[:], r8b[:], 0.0, None, ALU.is_ge))
                            c.v(lambda: nc.vector.tensor_tensor(r8c[:], r8b[:], iota8m_sb[:], ALU.mult))
                            c.v(lambda: nc.vector.tensor_scalar(r8c[:], r8c[:], 99.0, None, ALU.add))
                            c.v(lambda: nc.vector.tensor_reduce(cc[1][:], r8c[:], axis=AX.X, op=ALU.min))
                            c.v(lambda: nc.vector.tensor_scalar(oh1[:], iota8_sb[:], cc[1][:], None, ALU.subtract))
                            c.v(lambda: nc.vector.tensor_scalar(oh1[:], oh1[:], 0.0, None, ALU.is_equal))
                            c.v(lambda: nc.vector.scalar_tensor_tensor(r8b[:], oh1[:], -1e9, r8a[:], ALU.mult, ALU.add))
                            c.v(lambda: nc.vector.reduce_max(cc[2][:], r8b[:], axis=AX.X))
                            c.v(lambda: nc.vector.tensor_scalar(r8c[:], r8b[:], cc[2][:], None, ALU.subtract))
                            c.v(lambda: nc.vector.tensor_scalar(r8c[:], r8c[:], 0.0, None, ALU.is_ge))
                            c.v(lambda: nc.vector.tensor_tensor(r8c[:], r8c[:], iota8m_sb[:], ALU.mult))
                            c.v(lambda: nc.vector.tensor_scalar(r8c[:], r8c[:], 99.0, None, ALU.add))
                            c.v(lambda: nc.vector.tensor_reduce(cc[3][:], r8c[:], axis=AX.X, op=ALU.min))
                            c.v(lambda: nc.vector.tensor_scalar(oh2[:], iota8_sb[:], cc[3][:], None, ALU.subtract))
                            c.v(lambda: nc.vector.tensor_scalar(oh2[:], oh2[:], 0.0, None, ALU.is_equal))
                            c.v(lambda: nc.vector.tensor_tensor(cc[4][:], cc[0][:], cc[2][:], ALU.add))
                            c.v(lambda: nc.vector.tensor_scalar(cc[4][:], cc[4][:], 1e-9, None, ALU.add))
                            c.v(lambda: nc.vector.reciprocal(cc[4][:], cc[4][:]))
                            vohb = c.v(lambda: nc.vector.tensor_tensor(ohb[:], oh1[:], oh2[:], ALU.add))
                            if c.TE:
                                c.w("ve", vohb)
                                if t >= 2:
                                    c.w("ve", pdone[t - 2])
                            c.t(lambda: nc.tensor.matmul(
                                ps8[:, par * 512: par * 512 + 8], ltri_sb[:], ohb[:],
                                start=True, stop=True))
                            c.t(lambda: nc.tensor.matmul(
                                ps8[0:1, par * 512 + 16: par * 512 + 24], onescol_sb[:], ohb[:],
                                start=True, stop=True))
                            pmm = c.tmark(lambda: nc.tensor.matmul(
                                ps8[:, par * 512 + 32: par * 512 + 40], ones1_sb[:],
                                base_sb[0:1, :], start=True, stop=True))
                            if c.VE:
                                c.w("te", pmm)
                            c.v(lambda: nc.vector.tensor_copy(bvec[:], ps8[:, par * 512 + 32: par * 512 + 40]))
                            c.v(lambda: nc.vector.tensor_tensor(
                                slg[:], ps8[:, par * 512: par * 512 + 8], bvec[:], ALU.add))
                            pdone[t] = c.v(lambda: nc.vector.tensor_tensor(
                                base_sb[0:1, :], base_sb[0:1, :],
                                ps8[0:1, par * 512 + 16: par * 512 + 24], ALU.add))
                            c.v(lambda: nc.vector.tensor_tensor(r8b[:], slg[:], expbase_sb[:], ALU.add))
                            c.v(lambda: nc.vector.tensor_scalar(r8c[:], slg[:], float(C_LOC), None, ALU.is_ge))
                            c.v(lambda: nc.vector.scalar_tensor_tensor(r8b[:], r8c[:], 1e6, r8b[:], ALU.mult, ALU.add))
                            c.v(lambda: nc.vector.tensor_tensor(r8a[:], r8b[:], oh1[:], ALU.mult))
                            c.v(lambda: nc.vector.reduce_sum(cc[5][:], r8a[:], axis=AX.X))
                            c.v(lambda: nc.vector.tensor_copy(d1u[:, t:t + 1], cc[5][:]))
                            c.v(lambda: nc.vector.tensor_tensor(r8a[:], r8b[:], oh2[:], ALU.mult))
                            c.v(lambda: nc.vector.reduce_sum(cc[6][:], r8a[:], axis=AX.X))
                            c.v(lambda: nc.vector.tensor_copy(d2u[:, t:t + 1], cc[6][:]))
                            c.v(lambda: nc.vector.tensor_tensor(r8a[:], r8c[:], oh1[:], ALU.mult))
                            c.v(lambda: nc.vector.reduce_sum(cc[5][:], r8a[:], axis=AX.X))
                            c.v(lambda: nc.vector.tensor_scalar(cc[5][:], cc[5][:], -1.0, 1.0, ALU.mult, ALU.add))
                            c.v(lambda: nc.vector.tensor_tensor(cc[5][:], cc[5][:], cc[4][:], ALU.mult))
                            c.v(lambda: nc.vector.tensor_tensor(cc[5][:], cc[5][:], cc[0][:], ALU.mult))
                            c.v(lambda: nc.vector.tensor_copy(w1s[:, t:t + 1], cc[5][:]))
                            c.v(lambda: nc.vector.tensor_tensor(r8a[:], r8c[:], oh2[:], ALU.mult))
                            c.v(lambda: nc.vector.reduce_sum(cc[6][:], r8a[:], axis=AX.X))
                            c.v(lambda: nc.vector.tensor_scalar(cc[6][:], cc[6][:], -1.0, 1.0, ALU.mult, ALU.add))
                            c.v(lambda: nc.vector.tensor_tensor(cc[6][:], cc[6][:], cc[4][:], ALU.mult))
                            c.v(lambda: nc.vector.tensor_tensor(cc[6][:], cc[6][:], cc[2][:], ALU.mult))
                            vdw = c.v(lambda: nc.vector.tensor_copy(w2s[:, t:t + 1], cc[6][:]))
                            if c.GP:
                                c.w("ve", vdw)
                            c.dma(c.GP, lambda: e.indirect_dma_start(
                                out=xe_d, out_offset=IndirectOffsetOnAxis(d1u[:, t:t + 1], 0),
                                in_=xln[par][:], in_offset=None,
                                bounds_check=bc_reg, oob_is_err=False), "dma_g")
                            scat_done[t] = c.dma(c.GP, lambda: e.indirect_dma_start(
                                out=xe_d, out_offset=IndirectOffsetOnAxis(d2u[:, t:t + 1], 0),
                                in_=xln[par][:], in_offset=None,
                                bounds_check=bc_reg, oob_is_err=False), "dma_g")
                            emit_conv(2, upto_expert=0)
                        emit_conv(10**6, upto_expert=0)
                        c.v(lambda: nc.vector.memset(base_sb[0:1, :], 0.0))

                    with nc.Block(f"ffnA_{tag}") as blk:
                        run_phase(blk, emit_A)

                with ExitStack() as bsc:
                    bn = bsc.enter_context
                    xet = [bn(nc.sbuf_tensor(f"xet{i}_{tag}", [128, KH * ST], BF16)) for i in range(2)]
                    hT = bn(nc.sbuf_tensor(f"hT_{tag}", [128, MI * ST], BF16))
                    dwn = [bn(nc.sbuf_tensor(f"dwn{i}_{tag}", [128, HID], BF16)) for i in range(4)]
                    tsil = [bn(nc.sbuf_tensor(f"tsl{i}_{tag}", [128, ST], BF16)) for i in range(2)]
                    yesb = [bn(nc.sbuf_tensor(f"yes{i}_{tag}", [128, HID], BF16)) for i in range(2)]
                    psB = bn(nc.psum_tensor(f"psB_{tag}", [128, 4096], F32))
                    dbg_ps_sb = bn(nc.sbuf_tensor(f"dbgps_{tag}", [128, 3072], F32)) if tap == "dbg1" else None

                    def bank(b, n=512):
                        return psB[:, b * 512: b * 512 + n]

                    def emit_B(eng, e):
                        c = EC(eng, e)
                        c.barrier()
                        emit_conv = make_conv(c, e, sems, start_expert=1)
                        scat_target = sems.val("dma_g")
                        evv = {}
                        gu_done = {}
                        xet_loads = {}
                        dmm_prev = None
                        ye_ev = None
                        ye_last_store = {}

                        def issue_xet(j):
                            if j >= E * N_ST:
                                return
                            ex_, st_ = divmod(j, N_ST)
                            stn_ = ST_SIZES[st_]
                            s0_ = ex_ * C_LOC + st_ * ST
                            par_ = j % 2
                            if c.AC or c.SY:
                                c.w("dma_g", scat_target)
                                if (j - 2) in gu_done:
                                    c.w("te", gu_done[j - 2])
                            for f in range(KH):
                                xl_ = c.dma(c.AC, lambda f=f, stn_=stn_, s0_=s0_, par_=par_:
                                            e.dma_start(
                                                out=xet[par_][:, f * ST: f * ST + stn_],
                                                in_=xe_d[s0_: s0_ + stn_, f * 128:(f + 1) * 128],
                                                transpose=True), "dma_s")
                            xet_loads[j] = (sems.val("dma_s"), sems.val("dma_t"))

                        issue_xet(0)
                        for ex in range(E):
                            if gu_done:
                                last_j = max(gu_done)
                                if c.SY or c.GP:
                                    c.w("te", gu_done[last_j])
                            if first_call and (c.SY or c.GP):
                                c.w("dma_g", wconv[ex])
                            for res, src in ((gate_res, gate_bf), (up_res, up_bf)):
                                for k in range(KH):
                                    act, semn = (c.SY, "dma_w") if k % 2 == 0 else (c.GP, "dma_v")
                                    c.dma(act, lambda k=k, res=res, src=src: e.dma_start(
                                        out=res[:, k * I:(k + 1) * I],
                                        in_=src[ex, k * 128:(k + 1) * 128, :]), semn)
                            wl_s = sems.val("dma_w")
                            wl_v = sems.val("dma_v")
                            if first_call and ex + 1 < E:
                                emit_conv(10**6, upto_expert=ex + 1)

                            for st in range(N_ST):
                                j = ex * N_ST + st
                                stn = ST_SIZES[st]
                                smn = stn // 128
                                s0 = ex * C_LOC + st * ST
                                par = j % 2
                                issue_xet(j + 1)
                                if c.TE:
                                    c.w("dma_s", xet_loads[j][0])
                                    c.w("dma_t", xet_loads[j][1])
                                    c.w("dma_w", wl_s)
                                    c.w("dma_v", wl_v)
                                    if ye_ev is not None:
                                        c.w("ve", ye_ev)
                                for m in range(MI):
                                    gb, ub = (0, 1) if m % 2 == 0 else (2, 3)
                                    if c.TE and m >= 2:
                                        c.w("ve", evv[m - 2])
                                    for k in range(KH):
                                        c.t(lambda k=k, m=m, gb=gb, stn=stn, par=par: nc.tensor.matmul(
                                            bank(gb, stn),
                                            gate_res[:, k * I + m * 128: k * I + (m + 1) * 128],
                                            xet[par][:, k * ST: k * ST + stn],
                                            start=(k == 0), stop=(k == KH - 1)),
                                            inc=(k == KH - 1))
                                    gmm = sems.val("te")
                                    for k in range(KH):
                                        c.t(lambda k=k, m=m, ub=ub, stn=stn, par=par: nc.tensor.matmul(
                                            bank(ub, stn),
                                            up_res[:, k * I + m * 128: k * I + (m + 1) * 128],
                                            xet[par][:, k * ST: k * ST + stn],
                                            start=(k == 0), stop=(k == KH - 1)),
                                            inc=(k == KH - 1))
                                    umm = sems.val("te")
                                    if c.AC:
                                        c.w("te", gmm)
                                        if m >= 2:
                                            c.w("ve", evv[m - 2])
                                    asil = c.a(lambda m=m, gb=gb, stn=stn: nc.scalar.activation(
                                        tsil[m % 2][:, 0:stn], bank(gb, stn), AF.Silu))
                                    if c.VE:
                                        c.w("te", umm)
                                        c.w("ac", asil)
                                        if m < 2 and dmm_prev is not None:
                                            c.w("te", dmm_prev)
                                    evv[m] = c.v(lambda m=m, ub=ub, stn=stn: nc.vector.tensor_tensor(
                                        hT[:, m * ST: m * ST + stn], tsil[m % 2][:, 0:stn],
                                        bank(ub, stn), ALU.mult))
                                gu_done[j] = sems.val("te")
                                dl = {}
                                for k2 in range(MI):
                                    dpar = k2 % 4
                                    dsplit = (k2 % 2 == 0) or first_call
                                    act, semn = (c.SY, "dma_w") if dsplit else (c.GP, "dma_v")
                                    if (c.SY and dsplit) or (c.GP and not dsplit):
                                        if k2 < 4 and dmm_prev is not None:
                                            c.w("te", dmm_prev)
                                        if k2 >= 4:
                                            c.w("te", dl[k2 - 4][1])
                                    dv = c.dma(act, lambda k2=k2, dpar=dpar: e.dma_start(
                                        out=dwn[dpar][:],
                                        in_=down_bf[ex, k2 * 128:(k2 + 1) * 128, :]), semn)
                                    if c.TE:
                                        c.w(semn, dv)
                                        if k2 == 0:
                                            c.w("ve", evv[MI - 2])
                                            c.w("ve", evv[MI - 1])
                                    for sM in range(smn):
                                        for n2 in range(2):
                                            bk = sM * 2 + n2
                                            c.t(lambda k2=k2, sM=sM, n2=n2, bk=bk, dpar=dpar: nc.tensor.matmul(
                                                bank(bk),
                                                hT[:, k2 * ST + sM * 128: k2 * ST + sM * 128 + 128],
                                                dwn[dpar][:, n2 * 512:(n2 + 1) * 512],
                                                start=(k2 == 0), stop=(k2 == MI - 1)),
                                                inc=(sM == smn - 1 and n2 == 1))
                                    dl[k2] = (dv, sems.val("te"))
                                dmm_prev = sems.val("te")
                                if tap == "dbg1" and ex == DBG_EX and st == DBG_ST:
                                    if c.VE:
                                        c.w("te", dmm_prev)
                                    for bki in range(6):
                                        dpv = c.v(lambda bki=bki: nc.vector.tensor_copy(
                                            dbg_ps_sb[:, bki * 512:(bki + 1) * 512], bank(bki)))
                                    if c.AC:
                                        c.w("ve", dpv)
                                    c.dma(c.AC, lambda: e.dma_start(out=dbg_ps[:], in_=dbg_ps_sb[:]), "st")
                                for sM in range(smn):
                                    ypar = sM % 2
                                    if c.VE:
                                        c.w("te", dmm_prev)
                                        if ypar in ye_last_store:
                                            c.w("dma_s", ye_last_store[ypar])
                                    for n2 in range(2):
                                        yev = c.v(lambda sM=sM, n2=n2, ypar=ypar: nc.vector.tensor_copy(
                                            yesb[ypar][:, n2 * 512:(n2 + 1) * 512],
                                            bank(sM * 2 + n2)))
                                    if c.AC:
                                        c.w("ve", yev)
                                    ye_last_store[ypar] = c.dma(c.AC, lambda sM=sM, ypar=ypar: e.dma_start(
                                        out=ye_d[s0 + sM * 128: s0 + (sM + 1) * 128, :],
                                        in_=yesb[ypar][:]), "dma_s")
                                ye_ev = sems.val("ve")

                    with nc.Block(f"ffnB_{tag}") as blk:
                        run_phase(blk, emit_B)

                with ExitStack() as csc:
                    cn = csc.enter_context
                    xc = [cn(nc.sbuf_tensor(f"xc{i}_{tag}", [128, HID], F32)) for i in range(2)]
                    y1t = [cn(nc.sbuf_tensor(f"y1t{i}_{tag}", [128, HID], BF16)) for i in range(2)]
                    y2t = [cn(nc.sbuf_tensor(f"y2t{i}_{tag}", [128, HID], BF16)) for i in range(2)]
                    comb = [cn(nc.sbuf_tensor(f"comb{i}_{tag}", [128, HID], F32)) for i in range(2)]
                    sqc = cn(nc.sbuf_tensor(f"sqc_{tag}", [128, HID], F32))

                    def emit_C(eng, e):
                        c = EC(eng, e)
                        c.barrier()
                        bc_reg = e.to_reg(E * C_LOC - 1) if c.GP else None
                        cdone = {}
                        stdone = {}
                        for t in range(N_TILES):
                            par = t % 2
                            if c.SY and t >= 2:
                                c.w("ve", cdone[t - 2])
                            ldv = c.dma(c.SY, lambda: e.dma_start(
                                out=xc[par][:], in_=x_src[t * 128:(t + 1) * 128, :]), "dma_x")
                            if c.GP and t >= 2:
                                c.w("ve", cdone[t - 2])
                            c.dma(c.GP, lambda: e.indirect_dma_start(
                                out=y1t[par][:], out_offset=None,
                                in_=ye_d, in_offset=IndirectOffsetOnAxis(d1u[:, t:t + 1], 0),
                                bounds_check=bc_reg, oob_is_err=False), "dma_g")
                            gl = c.dma(c.GP, lambda: e.indirect_dma_start(
                                out=y2t[par][:], out_offset=None,
                                in_=ye_d, in_offset=IndirectOffsetOnAxis(d2u[:, t:t + 1], 0),
                                bounds_check=bc_reg, oob_is_err=False), "dma_g")
                            if c.VE:
                                c.w("dma_x", ldv)
                                c.w("dma_g", gl)
                                if t >= 2:
                                    c.w("st", stdone[t - 2])
                            c.v(lambda: nc.vector.tensor_scalar(
                                comb[par][:], y1t[par][:], w1s[:, t:t + 1], None, ALU.mult))
                            c.v(lambda: nc.vector.scalar_tensor_tensor(
                                comb[par][:], y2t[par][:], w2s[:, t:t + 1], comb[par][:], ALU.mult, ALU.add))
                            cv = c.v(lambda: nc.vector.scalar_tensor_tensor(
                                comb[par][:], comb[par][:], 0.5, xc[par][:], ALU.mult, ALU.add))
                            if stash_stats:
                                c.v(lambda: nc.vector.reduce_sum(s1_all[:, t:t + 1], comb[par][:], axis=AX.X))
                                c.v(lambda: nc.vector.tensor_tensor(sqc[:], comb[par][:], comb[par][:], ALU.mult))
                                cv = c.v(lambda: nc.vector.reduce_sum(s2_all[:, t:t + 1], sqc[:], axis=AX.X))
                            cdone[t] = cv
                            if c.AC:
                                c.w("ve", cv)
                            stdone[t] = c.dma(c.AC, lambda par=par: e.dma_start(
                                out=x_dst[t * 128:(t + 1) * 128, :], in_=comb[par][:]), "st")
                        if stash_stats:
                            c.v(lambda: nc.vector.tensor_scalar(m2_all[:], s1_all[:], 1.0 / HID, None, ALU.mult))
                            c.v(lambda: nc.vector.tensor_tensor(sqc[:, 0:N_TILES], m2_all[:], m2_all[:], ALU.mult))
                            c.v(lambda: nc.vector.tensor_scalar(r2_all[:], s2_all[:], 1.0 / HID, None, ALU.mult))
                            sv = c.v(lambda: nc.vector.tensor_sub(r2_all[:], r2_all[:], sqc[:, 0:N_TILES]))
                            if c.AC:
                                c.w("ve", sv)
                            c.a(lambda: nc.scalar.activation(r2_all[:], r2_all[:], AF.Ln, bias=eps5_sb[:]))
                            c.a(lambda: nc.scalar.activation(r2_all[:], r2_all[:], AF.Exp, scale=-0.5))

                    with nc.Block(f"ffnC_{tag}") as blk:
                        run_phase(blk, emit_C)

        def attn_sublayer():
            with ExitStack() as atc:
                at = atc.enter_context
                wo_res = at(nc.sbuf_tensor("wo_res", [128, KH * HID], BF16))
                with ExitStack() as psc:
                    pn = psc.enter_context
                    wq_res = pn(nc.sbuf_tensor("wq_res", [128, KH * HID], BF16))
                    wk_res = pn(nc.sbuf_tensor("wk_res", [128, KH * HID], BF16))
                    wv_res = pn(nc.sbuf_tensor("wv_res", [128, KH * HID], BF16))
                    wf_res = pn(nc.sbuf_tensor("wf_res", [128, KH * HID], BF16))
                    wg_res = pn(nc.sbuf_tensor("wg_res", [128, KH * HID], BF16))
                    wb_res = pn(nc.sbuf_tensor("wb_res", [128, KH * H], BF16))
                    dtb_bc = pn(nc.sbuf_tensor("dtb_bc", [128, HID], F32))
                    bg_bc = pn(nc.sbuf_tensor("bg_bc", [128, HID], F32))
                    nw_bc = pn(nc.sbuf_tensor("nw_bc", [128, HID], F32))
                    row_sb = pn(nc.sbuf_tensor("row_sb", [1, HID], F32))
                    xa2 = [pn(nc.sbuf_tensor(f"xa2{i}", [128, HID], F32)) for i in range(2)]
                    xln2 = pn(nc.sbuf_tensor("xln2", [128, HID], BF16))
                    lnT2 = pn(nc.sbuf_tensor("lnT2", [128, HID], BF16))
                    qt_s = [pn(nc.sbuf_tensor("qt_s{}".format(i), [128, HID], BF16)) for i in range(2)]
                    kt_s = [pn(nc.sbuf_tensor("kt_s{}".format(i), [128, HID], BF16)) for i in range(2)]
                    vt_s = [pn(nc.sbuf_tensor("vt_s{}".format(i), [128, HID], BF16)) for i in range(2)]
                    qst_s = [pn(nc.sbuf_tensor("qst_s{}".format(i), [128, HID], BF16)) for i in range(2)]
                    gt_s = [pn(nc.sbuf_tensor("gt_s{}".format(i), [128, HID], BF16)) for i in range(2)]
                    g2t_s = [pn(nc.sbuf_tensor("g2t_s{}".format(i), [128, HID], BF16)) for i in range(2)]
                    sig_s = pn(nc.sbuf_tensor("sig_s", [128, HID], F32))
                    sq2_s = pn(nc.sbuf_tensor("sq2_s", [128, HID], F32))
                    trev = [pn(nc.sbuf_tensor(f"trev{i}", [128, HID], BF16)) for i in range(2)]
                    aux_s = [pn(nc.sbuf_tensor("aux_s{}".format(i), [128, 64], F32)) for i in range(2)]
                    psP = pn(nc.psum_tensor("psP", [128, 3072], F32))
                    psPT = pn(nc.psum_tensor("psPT", [128, 2048], BF16))

                    def emit_P(eng, e):
                        c = EC(eng, e)
                        c.barrier()
                        wl = {}
                        for nm, res, src, width in (("q", wq_res, wq, HID), ("k", wk_res, wk, HID),
                                                    ("v", wv_res, wv, HID), ("b", wb_res, wb_in, H),
                                                    ("f", wf_res, wf, HID), ("g", wg_res, wg, HID),
                                                    ("o", wo_res, wo, HID)):
                            for k in range(KH):
                                wl[nm] = c.dma(c.GP, lambda nm=nm, res=res, src=src, width=width, k=k:
                                               e.dma_start(out=res[:, k * width:(k + 1) * width],
                                                           in_=src[k * 128:(k + 1) * 128, :]), "dma_g")
                        bdone = {}
                        for bi, (srcrow, dst) in enumerate(((dtb, dtb_bc), (bgv, bg_bc), (None, nw_bc))):
                            if srcrow is not None:
                                if c.SY and bi > 0:
                                    c.w("ve", bdone[bi - 1])
                                rv = c.dma(c.SY, lambda srcrow=srcrow: e.dma_start(
                                    out=row_sb[0:1, :], in_=srcrow[None, :]), "dma_x")
                                rsrc = row_sb
                            else:
                                rv = None
                                rsrc = nwrow_sb
                            if c.TE:
                                if rv is not None:
                                    c.w("dma_x", rv)
                                if bi > 0:
                                    c.w("ve", bdone[bi - 1])
                            for n2 in range(2):
                                tv = c.tmark(lambda n2=n2, rsrc=rsrc: nc.tensor.matmul(
                                    psP[:, n2 * 512:(n2 + 1) * 512], ones1_sb[:],
                                    rsrc[0:1, n2 * 512:(n2 + 1) * 512], start=True, stop=True))
                            if c.VE:
                                c.w("te", tv)
                            bdone[bi] = c.v(lambda dst=dst: nc.vector.tensor_copy(dst[:], psP[:, 0:1024]))
                        bc_done = bdone[2]

                        st_done = {}
                        trev_st = {}
                        tg_evac = {}
                        tg_count = 0
                        lnvs = {}
                        ldvs = {}
                        btas = {}
                        gvs = {}

                        def issue_x1(t2):
                            if t2 >= N_TILES:
                                return
                            if c.SY and t2 >= 2:
                                c.w("ve", lnvs[t2 - 2])
                            ldvs[t2] = c.dma(c.SY, lambda t2=t2: e.dma_start(
                                out=xa2[t2 % 2][:], in_=x1_d[t2 * 128:(t2 + 1) * 128, :]), "dma_x")

                        issue_x1(0)
                        for t in range(N_TILES):
                            par = t % 2
                            issue_x1(t + 1)
                            ldv = ldvs[t]
                            if c.VE:
                                c.w("dma_x", ldv)
                                if t >= 2:
                                    c.w("dma_x", st_done[t - 2])
                                    if (t % 2) in trev_st:
                                        c.w("dma_s", trev_st[t % 2])
                            lnv = c.v(lambda: nc.vector.tensor_scalar(
                                xln2[:], xa2[par][:], m2_all[:, t:t + 1], r2_all[:, t:t + 1],
                                ALU.subtract, ALU.mult))
                            lnvs[t] = lnv
                            rotp = tg_count % 2
                            if c.TE:
                                c.w("ve", lnv)
                                if tg_count >= 2:
                                    c.w("ve", tg_evac[tg_count - 2])
                            for f in range(KH):
                                c.t(lambda f=f, rotp=rotp: nc.tensor.transpose(
                                    psPT[:, rotp * 1024 + f * 128: rotp * 1024 + (f + 1) * 128],
                                    xln2[:, f * 128:(f + 1) * 128], ident_sb[:]),
                                    inc=(f == KH - 1))
                            tv = sems.val("te")
                            if c.VE:
                                c.w("te", tv)
                            lt = c.v(lambda rotp=rotp: nc.vector.tensor_copy(
                                lnT2[:], psPT[:, rotp * 1024: rotp * 1024 + 1024]))
                            tg_evac[tg_count] = lt
                            tg_count += 1
                            if c.TE:
                                c.w("ve", lt)
                                c.w("dma_g", wl["v"])
                                if t >= 1:
                                    c.w("ve", gvs[t - 1])
                                    c.w("ac", btas[t - 1])
                            sw1p = {}
                            for k in range(KH):
                                for pi, res in ((0, wq_res), (1, wk_res), (2, wv_res)):
                                    for n2 in range(2):
                                        c.t(lambda k=k, pi=pi, res=res, n2=n2: nc.tensor.matmul(
                                            psP[:, (pi * 2 + n2) * 512:(pi * 2 + n2) * 512 + 512],
                                            lnT2[:, k * 128:(k + 1) * 128],
                                            res[:, k * HID + n2 * 512: k * HID + (n2 + 1) * 512],
                                            start=(k == 0), stop=(k == KH - 1)),
                                            inc=(k == KH - 1))
                                        if k == KH - 1:
                                            sw1p[pi * 2 + n2] = sems.val("te")
                            sw1 = sems.val("te")
                            evm = {}
                            for pi, dst in ((0, qt_s[par]), (1, kt_s[par]), (2, vt_s[par])):
                                for n2 in range(2):
                                    bk = pi * 2 + n2
                                    if c.AC:
                                        c.w("te", sw1p[bk])
                                        if bk >= 2:
                                            c.w("ve", evm[bk - 2])
                                    sg = c.a(lambda bk=bk, n2=n2: nc.scalar.activation(
                                        sig_s[:, n2 * 512:(n2 + 1) * 512],
                                        psP[:, bk * 512: bk * 512 + 512], AF.Sigmoid))
                                    if c.VE:
                                        c.w("ac", sg)
                                    evm[bk] = c.v(lambda dst=dst, bk=bk, n2=n2: nc.vector.tensor_tensor(
                                        dst[:, n2 * 512:(n2 + 1) * 512],
                                        psP[:, bk * 512: bk * 512 + 512],
                                        sig_s[:, n2 * 512:(n2 + 1) * 512], ALU.mult))
                            qkv_ev = sems.val("ve")
                            if c.TE:
                                c.w("ve", qkv_ev)
                                c.w("dma_g", wl["g"])
                            for k in range(KH):
                                for pi, res in ((0, wf_res), (1, wg_res)):
                                    for n2 in range(2):
                                        c.t(lambda k=k, pi=pi, res=res, n2=n2: nc.tensor.matmul(
                                            psP[:, (pi * 2 + n2) * 512:(pi * 2 + n2) * 512 + 512],
                                            lnT2[:, k * 128:(k + 1) * 128],
                                            res[:, k * HID + n2 * 512: k * HID + (n2 + 1) * 512],
                                            start=(k == 0), stop=(k == KH - 1)))
                                c.t(lambda k=k: nc.tensor.matmul(
                                    psP[:, 4 * 512: 4 * 512 + H],
                                    lnT2[:, k * 128:(k + 1) * 128], wb_res[:, k * H:(k + 1) * H],
                                    start=(k == 0), stop=(k == KH - 1)), inc=(k == KH - 1))
                            sw2 = sems.val("te")
                            c.v(lambda par=par: nc.vector.tensor_tensor(sq2_s[:], qt_s[par][:], qt_s[par][:], ALU.mult))
                            c.v(lambda par=par: nc.vector.tensor_reduce(
                                aux_s[par][:, 16:32], sq2_s[:].rearrange("p (h d) -> p h d", h=H),
                                axis=AX.X, op=ALU.add))
                            c.v(lambda par=par: nc.vector.tensor_tensor(sq2_s[:], kt_s[par][:], kt_s[par][:], ALU.mult))
                            auxv = c.v(lambda par=par: nc.vector.tensor_reduce(
                                aux_s[par][:, 32:48], sq2_s[:].rearrange("p (h d) -> p h d", h=H),
                                axis=AX.X, op=ALU.add))
                            qsv = c.v(lambda par=par: nc.vector.tensor_tensor(
                                qst_s[par][:].rearrange("p (h d) -> p h d", h=H),
                                qt_s[par][:].rearrange("p (h d) -> p h d", h=H),
                                a_all[:].broadcast_to([128, H, D]), ALU.mult))
                            for nm, src, dst in (("q", qt_s[par], qT_d), ("k", kt_s[par], kT_d), ("qs", qst_s[par], qsT_d)):
                                rotp = tg_count % 2
                                if c.TE:
                                    c.w("ve", qsv)
                                    if tg_count >= 2:
                                        c.w("ve", tg_evac[tg_count - 2])
                                for f in range(KH):
                                    c.t(lambda f=f, src=src, rotp=rotp: nc.tensor.transpose(
                                        psPT[:, rotp * 1024 + f * 128: rotp * 1024 + (f + 1) * 128],
                                        src[:, f * 128:(f + 1) * 128], ident_sb[:]),
                                        inc=(f == KH - 1))
                                tv = sems.val("te")
                                if c.VE:
                                    c.w("te", tv)
                                    if tg_count >= 2 and (tg_count % 2) in trev_st:
                                        c.w("dma_s", trev_st[tg_count % 2])
                                tev = c.v(lambda rotp=rotp: nc.vector.tensor_copy(
                                    trev[rotp][:], psPT[:, rotp * 1024: rotp * 1024 + 1024]))
                                tg_evac[tg_count] = tev
                                if c.AC:
                                    c.w("ve", tev)
                                trev_st[rotp] = c.dma(c.AC, lambda dst=dst, rotp=rotp: e.dma_start(
                                    out=dst[t].rearrange("f p t2 -> p f t2"),
                                    in_=trev[rotp][:].rearrange("p (f t2) -> p f t2", f=KH)), "dma_s")
                                tg_count += 1
                            if c.AC:
                                c.w("te", sw2)
                            bta = c.a(lambda par=par: nc.scalar.activation(
                                aux_s[par][:, 0:H], psP[:, 4 * 512: 4 * 512 + H], AF.Sigmoid))
                            btas[t] = bta
                            if c.VE:
                                c.w("te", sw2)
                                c.w("ve", bc_done)
                            c.v(lambda: nc.vector.tensor_tensor(
                                sq2_s[:, 0:512], psP[:, 0:512], dtb_bc[:, 0:512], ALU.subtract))
                            fv = c.v(lambda: nc.vector.tensor_tensor(
                                sq2_s[:, 512:1024], psP[:, 512:1024], dtb_bc[:, 512:1024], ALU.subtract))
                            if c.AC:
                                c.w("ve", fv)
                            ga = c.a(lambda par=par: nc.scalar.activation(gt_s[par][:], sq2_s[:], AF.Sigmoid))
                            c.v(lambda: nc.vector.tensor_tensor(
                                sig_s[:, 0:512], psP[:, 1024:1536], bg_bc[:, 0:512], ALU.add))
                            gv = c.v(lambda: nc.vector.tensor_tensor(
                                sig_s[:, 512:1024], psP[:, 1536:2048], bg_bc[:, 512:1024], ALU.add))
                            gvs[t] = gv
                            if c.AC:
                                c.w("ve", gv)
                            g2a = c.a(lambda: nc.scalar.activation(sq2_s[:], sig_s[:], AF.Sigmoid))
                            if c.VE:
                                c.w("ac", g2a)
                            g2v = c.v(lambda par=par: nc.vector.tensor_tensor(g2t_s[par][:], sq2_s[:], nw_bc[:], ALU.mult))
                            if c.SY:
                                c.w("ve", max(qsv, g2v, auxv))
                                c.w("ac", max(bta, ga))
                            for src, dst in ((kt_s[par], kk_d), (vt_s[par], vv_d), (gt_s[par], gg_d), (g2t_s[par], gg2_d)):
                                c.dma(c.SY, lambda src=src, dst=dst: e.dma_start(
                                    out=dst[t], in_=src[:]), "dma_x")
                            sd = c.dma(c.SY, lambda par=par: e.dma_start(out=aux_d[t], in_=aux_s[par][:]), "dma_x")
                            st_done[t] = sd

                    with nc.Block("attnP") as blk:
                        run_phase(blk, emit_P)

                with ExitStack() as ssc:
                    sn = ssc.enter_context
                    cb = {}
                    for nm in ("qT", "qsT", "kT", "kk", "vv", "gg", "gg2"):
                        cb[nm] = [sn(nc.sbuf_tensor(f"cb_{nm}{i}", [128, HID], BF16)) for i in range(2)]
                    cb_aux = [sn(nc.sbuf_tensor(f"cbaux{i}", [128, 64], F32)) for i in range(2)]
                    x1t = [sn(nc.sbuf_tensor(f"x1t{i}", [128, HID], F32)) for i in range(2)]
                    vB = sn(nc.sbuf_tensor("vB", [128, HID], BF16))
                    ksc_t = sn(nc.sbuf_tensor("ksc_t", [128, HID], BF16))
                    smv = sn(nc.sbuf_tensor("smv", [128, 1024], BF16))
                    gq = sn(nc.sbuf_tensor("gq", [128, HID], F32))
                    o3 = sn(nc.sbuf_tensor("o3", [128, HID], F32))
                    sqo = sn(nc.sbuf_tensor("sqo", [128, HID], F32))
                    o5 = [sn(nc.sbuf_tensor(f"o5_{i}", [128, HID], BF16)) for i in range(2)]
                    oTe = sn(nc.sbuf_tensor("oTe", [128, HID], BF16))
                    Sst = [[sn(nc.sbuf_tensor(f"Sst{s}_{i}", [64, H * D], F32)) for i in range(2)]
                           for s in range(2)]
                    Sbf = [sn(nc.sbuf_tensor(f"Sbf{s}", [128, H * D], BF16)) for s in range(2)]
                    sscr = sn(nc.sbuf_tensor("sscr", [64, H * D], F32))
                    r16 = [sn(nc.sbuf_tensor(f"r16_{i}", [128, 16], F32)) for i in range(4)]
                    combo = [sn(nc.sbuf_tensor(f"combo{i}", [128, HID], F32)) for i in range(2)]
                    psS = sn(nc.psum_tensor("psS", [128, 4096], F32))

                    def emit_S(eng, e):
                        c = EC(eng, e)
                        c.barrier()
                        po_done = {}
                        su_done = {}
                        heads_done = {}
                        sbf_done = {}
                        st2_done = {}
                        ov_done = {}
                        otv_done = {}
                        o3r = {}
                        tvs = {}
                        lvs = {}

                        def emit_tail(jt):
                            dsetj = jt % 2
                            parj = jt % 2
                            gcj = (jt % 2) * N_CHUNK + jt // 2
                            base2 = (2048 + dsetj * 1024) * 2
                            if c.TE:
                                c.w("ve", ov_done[jt])
                            for f in range(KH):
                                c.t(lambda f=f, base2=base2, parj=parj: nc.tensor.transpose(
                                    psS.bitcast(BF16)[:, base2 + f * 128: base2 + (f + 1) * 128],
                                    o5[parj][:, f * 128:(f + 1) * 128], ident_sb[:]),
                                    inc=(f == KH - 1))
                            tvs[jt] = sems.val("te")
                            if c.VE:
                                c.w("te", tvs[jt])
                            otv_done[jt] = c.v(lambda base2=base2: nc.vector.tensor_copy(
                                oTe[:], psS.bitcast(BF16)[:, base2: base2 + 1024]))
                            if c.TE:
                                c.w("ve", otv_done[jt])
                            for n2 in range(2):
                                for k in range(KH):
                                    c.t(lambda n2=n2, k=k, dsetj=dsetj: nc.tensor.matmul(
                                        psS[:, 2048 + dsetj * 1024 + n2 * 512: 2048 + dsetj * 1024 + n2 * 512 + 512],
                                        oTe[:, k * 128:(k + 1) * 128],
                                        wo_res[:, k * HID + n2 * 512: k * HID + (n2 + 1) * 512],
                                        start=(k == 0), stop=(k == KH - 1)),
                                        inc=(n2 == 1 and k == KH - 1))
                            wv_ = sems.val("te")
                            if c.VE:
                                c.w("te", wv_)
                                c.w("dma_x", lvs[jt])
                                if jt >= 2:
                                    c.w("dma_s", st2_done[jt - 2])
                            c.v(lambda dsetj=dsetj, parj=parj: nc.vector.tensor_add(
                                combo[parj][:, 0:512],
                                psS[:, 2048 + dsetj * 1024: 2048 + dsetj * 1024 + 512],
                                x1t[parj][:, 0:512]))
                            cv = c.v(lambda dsetj=dsetj, parj=parj: nc.vector.tensor_add(
                                combo[parj][:, 512:1024],
                                psS[:, 2048 + dsetj * 1024 + 512: 2048 + dsetj * 1024 + 1024],
                                x1t[parj][:, 512:1024]))
                            po_done[jt] = cv
                            if c.AC:
                                c.w("ve", cv)
                            st2_done[jt] = c.dma(c.AC, lambda parj=parj, gcj=gcj: e.dma_start(
                                out=x2_d[gcj * 128:(gcj + 1) * 128, :], in_=combo[parj][:]), "dma_s")

                        for it in range(N_TILES):
                            sq = it % 2
                            ch = it // 2
                            gc = sq * N_CHUNK + ch
                            par = it % 2
                            dset = sq
                            if c.SY and it >= 2:
                                c.w("ve", po_done[it - 2])
                            for nm, src in (("qT", qT_d), ("qsT", qsT_d), ("kT", kT_d)):
                                c.dma(c.SY, lambda nm=nm, src=src: e.dma_start(
                                    out=cb[nm][par][:].rearrange("p (f t2) -> p f t2", f=KH),
                                    in_=src[gc].rearrange("f p t2 -> p f t2")), "dma_x")
                            for nm, src in (("kk", kk_d), ("vv", vv_d), ("gg", gg_d), ("gg2", gg2_d)):
                                c.dma(c.SY, lambda nm=nm, src=src: e.dma_start(
                                    out=cb[nm][par][:], in_=src[gc]), "dma_x")
                            c.dma(c.SY, lambda: e.dma_start(out=cb_aux[par][:], in_=aux_d[gc]), "dma_x")
                            lv = c.dma(c.SY, lambda: e.dma_start(
                                out=x1t[par][:], in_=x1_d[gc * 128:(gc + 1) * 128, :]), "dma_x")
                            lvs[it] = lv
                            if c.AC:
                                c.w("dma_x", lv)
                            c.a(lambda: nc.scalar.activation(r16[0][:], cb_aux[par][:, 16:32], AF.Ln, bias=eps24_sb[:]))
                            c.a(lambda: nc.scalar.activation(r16[0][:], r16[0][:], AF.Exp, scale=-0.5))
                            c.a(lambda: nc.scalar.activation(r16[1][:], cb_aux[par][:, 32:48], AF.Ln, bias=eps24_sb[:]))
                            rka = c.a(lambda: nc.scalar.activation(r16[1][:], r16[1][:], AF.Exp, scale=-0.5))
                            if c.VE:
                                c.w("ac", rka)
                                if it >= 1:
                                    c.w("te", heads_done[it - 1])
                            c.v(lambda: nc.vector.tensor_tensor(r16[2][:], r16[1][:], cb_aux[par][:, 0:16], ALU.mult))
                            c.v(lambda: nc.vector.tensor_tensor(
                                vB[:].rearrange("p (h d) -> p h d", h=H),
                                cb["vv"][par][:].rearrange("p (h d) -> p h d", h=H),
                                r16[2][:].broadcast_to([128, H, D]), ALU.mult))
                            c.v(lambda: nc.vector.tensor_tensor(
                                ksc_t[:].rearrange("p (h d) -> p h d", h=H),
                                cb["kk"][par][:].rearrange("p (h d) -> p h d", h=H),
                                kdec_all[:].broadcast_to([128, H, D]), ALU.mult))
                            prep = c.v(lambda: nc.vector.tensor_tensor(
                                gq[:].rearrange("p (h d) -> p h d", h=H),
                                cb["gg"][par][:].rearrange("p (h d) -> p h d", h=H),
                                r16[0][:].broadcast_to([128, H, D]), ALU.mult))
                            if ch == 0:
                                c.v(lambda sq=sq: nc.vector.memset(Sst[sq][0][:], 0.0))
                                prep = c.v(lambda sq=sq: nc.vector.memset(Sbf[sq][:], 0.0))
                            
                            if c.TE:
                                c.w("ve", prep)
                                c.w("dma_x", lv)
                                if it >= 2:
                                    c.w("ac", sbf_done[it - 2])
                                    c.w("ve", po_done[it - 2])
                                    c.w("ve", su_done[it - 2])
                                if it >= 1:
                                    c.w("ve", o3r[it - 1])
                            msk_done = {}
                            for h0 in range(0, H, 2):
                                mm1p = {}
                                for dh in range(2):
                                    h = h0 + dh
                                    fb = h // 2
                                    pb = (h % 2) * 64
                                    if c.TE and h0 >= 2 and dh == 0:
                                        c.w("ve", msk_done[h0 - 1])
                                    mm1p[dh] = c.tmark(lambda h=h, fb=fb, pb=pb, dh=dh: nc.tensor.matmul(
                                        psS[:, 1024 + dh * 512: 1024 + dh * 512 + 128],
                                        cb["kT"][par][pb:pb + 64, fb * 128:(fb + 1) * 128],
                                        cb["qT"][par][pb:pb + 64, fb * 128:(fb + 1) * 128],
                                        start=True, stop=True))
                                if c.VE:
                                    c.w("te", mm1p[1])
                                for dh in range(2):
                                    h = h0 + dh
                                    msk_done[h] = c.v(lambda h=h, dh=dh: nc.vector.tensor_tensor(
                                        smv[:, dh * 128:(dh + 1) * 128],
                                        psS[:, 1024 + dh * 512: 1024 + dh * 512 + 128],
                                        dt_all[:, h * 128:(h + 1) * 128], ALU.mult))
                                for dh in range(2):
                                    h = h0 + dh
                                    fb = h // 2
                                    pb = (h % 2) * 64
                                    if c.TE:
                                        c.w("ve", msk_done[h])
                                    c.t(lambda h=h, dh=dh: nc.tensor.matmul(
                                        psS[:, h * 64:(h + 1) * 64],
                                        smv[:, dh * 128:(dh + 1) * 128],
                                        vB[:, h * 64:(h + 1) * 64], start=True, stop=False))
                                    c.t(lambda h=h, fb=fb, pb=pb, sq=sq: nc.tensor.matmul(
                                        psS[:, h * 64:(h + 1) * 64],
                                        cb["qsT"][par][pb:pb + 64, fb * 128:(fb + 1) * 128],
                                        Sbf[sq][pb:pb + 64, h * 64:(h + 1) * 64], start=False, stop=True))
                                    c.t(lambda h=h, dset=dset: nc.tensor.matmul(
                                        psS[0:64, 2048 + dset * 1024 + h * 64: 2048 + dset * 1024 + (h + 1) * 64],
                                        ksc_t[:, h * 64:(h + 1) * 64],
                                        vB[:, h * 64:(h + 1) * 64], start=True, stop=True),
                                        inc=(h == H - 1))
                            heads_done[it] = sems.val("te")
                            if c.VE:
                                c.w("te", heads_done[it])
                            c.v(lambda sq=sq, ch=ch: nc.vector.tensor_tensor(
                                sscr[:].rearrange("p (h d) -> p h d", h=H),
                                Sst[sq][ch % 2][:].rearrange("p (h d) -> p h d", h=H),
                                a128_all[0:64, :].broadcast_to([64, H, D]), ALU.mult))
                            su = c.v(lambda dset=dset, sq=sq, ch=ch: nc.vector.tensor_add(
                                Sst[sq][(ch + 1) % 2][:],
                                sscr[:],
                                psS[0:64, 2048 + dset * 1024: 2048 + dset * 1024 + 1024]))
                            su_done[it] = su
                            if c.AC:
                                c.w("ve", su)
                            c.a(lambda sq=sq, ch=ch: nc.scalar.copy(Sbf[sq][0:64, :], Sst[sq][(ch + 1) % 2][:]))
                            sbf_done[it] = c.a(lambda sq=sq, ch=ch: nc.scalar.copy(Sbf[sq][64:128, :], Sst[sq][(ch + 1) % 2][:]))
                            o3r[it] = c.v(lambda: nc.vector.tensor_tensor(
                                o3[:], psS[:, 0:1024], gq[:], ALU.mult))
                            c.v(lambda: nc.vector.tensor_tensor(sqo[:], o3[:], o3[:], ALU.mult))
                            msv = c.v(lambda: nc.vector.tensor_reduce(
                                r16[3][:], sqo[:].rearrange("p (h d) -> p h d", h=H),
                                axis=AX.X, op=ALU.add))
                            if c.AC:
                                c.w("ve", msv)
                            c.a(lambda: nc.scalar.activation(r16[3][:], r16[3][:], AF.Ln,
                                                             scale=1.0 / D, bias=eps5_sb[:]))
                            rst = c.a(lambda: nc.scalar.activation(r16[3][:], r16[3][:], AF.Exp, scale=-0.5))
                            if c.VE:
                                c.w("ac", rst)
                            c.v(lambda: nc.vector.tensor_tensor(
                                o3[:].rearrange("p (h d) -> p h d", h=H),
                                o3[:].rearrange("p (h d) -> p h d", h=H),
                                r16[3][:].broadcast_to([128, H, D]), ALU.mult))
                            if c.VE and it >= 2:
                                c.w("te", tvs[it - 2])
                            ov_done[it] = c.v(lambda par=par: nc.vector.tensor_tensor(
                                o5[par][:], o3[:], cb["gg2"][par][:], ALU.mult))
                            if it >= 1:
                                emit_tail(it - 1)
                        emit_tail(N_TILES - 1)

                    with nc.Block("attnS") as blk:
                        run_phase(blk, emit_S)

        if "ffn1" in phases:
            dst1 = yout if tap == "x1" else x1_d
            ffn_sublayer("f1", xin, dst1, stash_stats=True, first_call=True)
        if "attn" in phases:
            attn_sublayer()
        if "ffn2" in phases:
            ffn_sublayer("f2", x2_d, yout, stash_stats=False,
                         first_call=("ffn1" not in phases))
        if tap == "x2" and "attn" in phases:
            def emit_cp(eng, e):
                c = EC(eng, e)
                c.barrier()
                for t in range(N_TILES):
                    c.dma(c.SY, lambda t=t: e.dma_start(
                        out=yout[t * 128:(t + 1) * 128, :],
                        in_=x2_d[t * 128:(t + 1) * 128, :]), "st")
            with nc.Block("cp") as blk:
                run_phase(blk, emit_cp)

        if tap == "dbg1":
            def emit_dbg(eng, e):
                c = EC(eng, e)
                c.barrier()
                c.dma(c.SY, lambda: e.dma_start(out=dbg_w[:, 0:N_TILES], in_=w1s[:]), "st")
                c.dma(c.SY, lambda: e.dma_start(out=dbg_w[:, N_TILES:2 * N_TILES], in_=w2s[:]), "st")
                c.dma(c.SY, lambda: e.dma_start(
                    out=dbg_w.bitcast(U32)[:, 2 * N_TILES:3 * N_TILES], in_=d1u[:]), "st")
                c.dma(c.SY, lambda: e.dma_start(
                    out=dbg_w.bitcast(U32)[:, 3 * N_TILES:4 * N_TILES], in_=d2u[:]), "st")
                for t in range(E * C_LOC // 128):
                    c.dma(c.GP, lambda t=t: e.dma_start(
                        out=dbg_xe[t * 128:(t + 1) * 128, :], in_=xe_d[t * 128:(t + 1) * 128, :]), "st")
                    c.dma(c.GP, lambda t=t: e.dma_start(
                        out=dbg_ye[t * 128:(t + 1) * 128, :], in_=ye_d[t * 128:(t + 1) * 128, :]), "st")
                for t in range(I // 128):
                    c.dma(c.GP, lambda t=t: e.dma_start(
                        out=dbg_dwn[t * 128:(t + 1) * 128, :],
                        in_=down_bf[DBG_EX, t * 128:(t + 1) * 128, :]), "st")
            with nc.Block("dbg") as blk:
                run_phase(blk, emit_dbg)

        def emit_fin(eng, e):
            c = EC(eng, e)
            if c.SY:
                c.barrier()
        with nc.Block("fin") as blk:
            run_phase(blk, emit_fin)

    return nc


_PROG_CACHE = {}


def _get_program(phases, tap):
    key = (tuple(phases), tap)
    if key not in _PROG_CACHE:
        _PROG_CACHE[key] = build_program(phases=tuple(phases), tap=tap)
    return _PROG_CACHE[key]


def kernel(L0_attn_x, L0_attn_Wq, L0_attn_Wk, L0_attn_Wv, L0_attn_Wf, L0_attn_Wb,
           L0_attn_A_log, L0_attn_dt_bias, L0_attn_Wg, L0_attn_bg, L0_attn_Wo,
           L0_attn_norm_w, L0_ffn_router_gate, L0_ffn_gate_w, L0_ffn_up_w,
           L0_ffn_down_w, _phases=("ffn1", "attn", "ffn2"), _tap=None):
    x = np.ascontiguousarray(np.asarray(L0_attn_x, dtype=np.float32))
    full = {
        "Wq": np.ascontiguousarray(np.asarray(L0_attn_Wq, np.float32)),
        "Wk": np.ascontiguousarray(np.asarray(L0_attn_Wk, np.float32)),
        "Wv": np.ascontiguousarray(np.asarray(L0_attn_Wv, np.float32)),
        "Wf": np.ascontiguousarray(np.asarray(L0_attn_Wf, np.float32)),
        "Wb": np.ascontiguousarray(np.asarray(L0_attn_Wb, np.float32)),
        "A_log": np.ascontiguousarray(np.asarray(L0_attn_A_log, np.float32)),
        "dt_bias": np.ascontiguousarray(np.asarray(L0_attn_dt_bias, np.float32)),
        "Wg": np.ascontiguousarray(np.asarray(L0_attn_Wg, np.float32)),
        "bg": np.ascontiguousarray(np.asarray(L0_attn_bg, np.float32)),
        "Wo": np.ascontiguousarray(np.asarray(L0_attn_Wo, np.float32)),
        "norm_w": np.ascontiguousarray(np.asarray(L0_attn_norm_w, np.float32)),
        "router_gate": np.ascontiguousarray(np.asarray(L0_ffn_router_gate, np.float32)),
        "gate_w": np.ascontiguousarray(np.asarray(L0_ffn_gate_w, np.float32)),
        "up_w": np.ascontiguousarray(np.asarray(L0_ffn_up_w, np.float32)),
        "down_w": np.ascontiguousarray(np.asarray(L0_ffn_down_w, np.float32)),
    }
    nc = _get_program(_phases, _tap)
    in_maps = []
    for cix in range(N_CORES):
        m = dict(full)
        m["x_sh"] = np.ascontiguousarray(x[cix * B_CORE:(cix + 1) * B_CORE].reshape(T_LOC, HID))
        in_maps.append(m)
    res = run_bass_kernel_spmd(nc, in_maps, core_ids=list(range(N_CORES)))
    out = np.concatenate([r["y_sh"].reshape(B_CORE, S, HID) for r in res.results], axis=0)
    return out


# version 09 (final)
# speedup vs baseline: 1.0507x; 1.0507x over previous
from contextlib import ExitStack

import numpy as np
import ml_dtypes
import concourse.bass as bass
import concourse.mybir as mybir
from concourse.bass import IndirectOffsetOnAxis
from concourse.bass_utils import run_bass_kernel_spmd

F32 = mybir.dt.float32
BF16 = mybir.dt.bfloat16
U32 = mybir.dt.uint32
AF = mybir.ActivationFunctionType
ALU = mybir.AluOpType
AX = mybir.AxisListType

N_CORES = 8
B, S, HID = 16, 4096, 1024
H, D = 16, 64
E, I = 8, 4096
B_CORE = B // N_CORES
T_LOC = B_CORE * S
N_TILES = T_LOC // 128
N_CHUNK = S // 128
C_LOC = 2304
ST = 512
ST_SIZES = (512, 512, 512, 512, 256)
N_ST = len(ST_SIZES)
KH = HID // 128
MI = I // 128
RMS_EPS = 1e-5
L2_EPS = 1e-24
LN_EPS = 1e-5


def _consts():
    j = np.arange(128)[:, None].astype(np.float32)
    i = np.arange(128)[None, :].astype(np.float32)
    c = {}
    c["ident"] = np.eye(128, dtype=ml_dtypes.bfloat16)
    c["identf"] = np.eye(128, dtype=np.float32)
    c["onescolf"] = np.ones((128, 1), dtype=np.float32)
    c["Mpos"] = np.maximum(i - 1.0 - j, 0.0).astype(np.float32)
    c["mask"] = (j < i).astype(np.float32)
    c["Ltri"] = (j < i).astype(ml_dtypes.bfloat16)
    c["onescol"] = np.ones((128, 1), dtype=ml_dtypes.bfloat16)
    c["ones1"] = np.ones((1, 128), dtype=np.float32)
    c["iotap"] = np.ascontiguousarray(j)
    c["iota127"] = np.ascontiguousarray(127.0 - j)
    c["iota8"] = np.tile(np.arange(8, dtype=np.float32)[None, :], (128, 1))
    c["iota8m"] = np.tile(np.arange(8, dtype=np.float32)[None, :] - 99.0, (128, 1))
    c["expbase"] = np.tile((np.arange(8, dtype=np.float32) * C_LOC)[None, :], (128, 1))
    return c


class _Sems:
    def __init__(self):
        self.h = {}
        self.n = {}

    def add(self, name, handle):
        self.h[name] = handle
        self.n[name] = 0

    def snap(self):
        return dict(self.n)

    def restore(self, s):
        self.n = dict(s)

    def inc(self, instr, name, k):
        self.n[name] += k
        if instr is not None:
            instr.then_inc(self.h[name], k)
        return self.n[name]

    def val(self, name):
        return self.n[name]


SEM_NAMES = ("dma_x", "dma_w", "dma_s", "dma_g", "dma_v", "dma_t", "te", "ve", "ac", "pl", "st")


def build_program(phases=("ffn1", "attn", "ffn2"), tap=None):
    nc = bass.Bass("TRN2", target_bir_lowering=False, debug=False)

    xin = nc.dram_tensor("x_sh", [T_LOC, HID], F32, kind="ExternalInput").ap()
    wq = nc.dram_tensor("Wq", [HID, HID], F32, kind="ExternalInput").ap()
    wk = nc.dram_tensor("Wk", [HID, HID], F32, kind="ExternalInput").ap()
    wv = nc.dram_tensor("Wv", [HID, HID], F32, kind="ExternalInput").ap()
    wf = nc.dram_tensor("Wf", [HID, HID], F32, kind="ExternalInput").ap()
    wb_in = nc.dram_tensor("Wb", [HID, H], F32, kind="ExternalInput").ap()
    alog = nc.dram_tensor("A_log", [H], F32, kind="ExternalInput").ap()
    dtb = nc.dram_tensor("dt_bias", [HID], F32, kind="ExternalInput").ap()
    wg = nc.dram_tensor("Wg", [HID, HID], F32, kind="ExternalInput").ap()
    bgv = nc.dram_tensor("bg", [HID], F32, kind="ExternalInput").ap()
    wo = nc.dram_tensor("Wo", [HID, HID], F32, kind="ExternalInput").ap()
    nw = nc.dram_tensor("norm_w", [D], F32, kind="ExternalInput").ap()
    rgate = nc.dram_tensor("router_gate", [HID, E], F32, kind="ExternalInput").ap()
    gate_w = nc.dram_tensor("gate_w", [E, HID, I], F32, kind="ExternalInput").ap()
    up_w = nc.dram_tensor("up_w", [E, HID, I], F32, kind="ExternalInput").ap()
    down_w = nc.dram_tensor("down_w", [E, I, HID], F32, kind="ExternalInput").ap()
    yout = nc.dram_tensor("y_sh", [T_LOC, HID], F32, kind="ExternalOutput").ap()
    if tap == "dbg1":
        dbg_w = nc.dram_tensor("dbg_w", [128, 4 * N_TILES], F32, kind="ExternalOutput").ap()
        dbg_xe = nc.dram_tensor("dbg_xe", [E * C_LOC, HID], BF16, kind="ExternalOutput").ap()
        dbg_ye = nc.dram_tensor("dbg_ye", [E * C_LOC, HID], BF16, kind="ExternalOutput").ap()
        dbg_ht = nc.dram_tensor("dbg_ht", [128, MI * ST], BF16, kind="ExternalOutput").ap()
        dbg_xet = nc.dram_tensor("dbg_xet", [128, KH * ST], BF16, kind="ExternalOutput").ap()
        dbg_ps = nc.dram_tensor("dbg_ps", [128, 3072], F32, kind="ExternalOutput").ap()
        dbg_dwn = nc.dram_tensor("dbg_dwn", [I, HID], BF16, kind="ExternalOutput").ap()
    DBG_EX, DBG_ST = 0, 2

    gate_bf = nc.dram_tensor("gate_bf", [E, HID, I], BF16, kind="Internal").ap()
    up_bf = nc.dram_tensor("up_bf", [E, HID, I], BF16, kind="Internal").ap()
    down_bf = nc.dram_tensor("down_bf", [E, I, HID], BF16, kind="Internal").ap()
    xe_d = nc.dram_tensor("XE", [E * C_LOC, HID], BF16, kind="Internal").ap()
    ye_d = nc.dram_tensor("YE", [E * C_LOC, HID], BF16, kind="Internal").ap()
    x1_d = nc.dram_tensor("x1_d", [T_LOC, HID], F32, kind="Internal").ap()
    x2_d = nc.dram_tensor("x2_d", [T_LOC, HID], F32, kind="Internal").ap()
    qT_d = nc.dram_tensor("qT_d", [N_TILES, KH, 128, 128], BF16, kind="Internal").ap()
    qsT_d = nc.dram_tensor("qsT_d", [N_TILES, KH, 128, 128], BF16, kind="Internal").ap()
    kT_d = nc.dram_tensor("kT_d", [N_TILES, KH, 128, 128], BF16, kind="Internal").ap()
    kk_d = nc.dram_tensor("kk_d", [N_TILES, 128, HID], BF16, kind="Internal").ap()
    vv_d = nc.dram_tensor("vv_d", [N_TILES, 128, HID], BF16, kind="Internal").ap()
    gg_d = nc.dram_tensor("gg_d", [N_TILES, 128, HID], BF16, kind="Internal").ap()
    gg2_d = nc.dram_tensor("gg2_d", [N_TILES, 128, HID], BF16, kind="Internal").ap()
    aux_d = nc.dram_tensor("aux_d", [N_TILES, 128, 64], F32, kind="Internal").ap()

    cdat = _consts()
    cd = {}
    for k, v in cdat.items():
        if v.dtype == ml_dtypes.bfloat16:
            cd[k] = nc.inline_tensor(np.ascontiguousarray(v.view(np.uint16)), f"c_{k}")
        else:
            cd[k] = nc.inline_tensor(np.ascontiguousarray(v), f"c_{k}")

    sems = _Sems()

    def run_phase(blk, emitfn):
        snap = sems.snap()
        for name, dec in (("sync", blk.sync), ("tensor", blk.tensor),
                          ("vector", blk.vector), ("scalar", blk.scalar),
                          ("gpsimd", blk.gpsimd)):
            sems.restore(snap)
            dec(lambda e, _n=name: emitfn(_n, e))

    class EC:

        def __init__(self, eng, e):
            self.eng = eng
            self.e = e
            self.SY = eng == "sync"
            self.TE = eng == "tensor"
            self.VE = eng == "vector"
            self.AC = eng == "scalar"
            self.GP = eng == "gpsimd"

        def barrier(self):
            for s in SEM_NAMES:
                if sems.val(s) > 0:
                    self.e.wait_ge(sems.h[s], sems.val(s))

        def v(self, f):
            if self.VE:
                self.e.wait_ge(sems.h["ve"], sems.val("ve"))
                i = f()
            else:
                i = None
            return sems.inc(i, "ve", 1)

        def a(self, f):
            if self.AC:
                self.e.wait_ge(sems.h["ac"], sems.val("ac"))
                i = f()
            else:
                i = None
            return sems.inc(i, "ac", 1)

        def p(self, f):
            if self.GP:
                self.e.wait_ge(sems.h["pl"], sems.val("pl"))
                i = f()
            else:
                i = None
            return sems.inc(i, "pl", 1)

        def t(self, f, inc=False):
            i = f() if self.TE else None
            if inc:
                return sems.inc(i, "te", 1)
            sems.n["te"] += 0
            return None

        def tmark(self, f):
            return self.t(f, inc=True)

        def dma(self, active, f, sem):
            i = f() if active else None
            return sems.inc(i, sem, 16)

        def w(self, name, val=None):
            self.e.wait_ge(sems.h[name], sems.val(name) if val is None else val)

    with ExitStack() as top:
        ent = top.enter_context
        for s in SEM_NAMES:
            sems.add(s, ent(nc.semaphore(s)))

        ident_sb = ent(nc.sbuf_tensor("ident_sb", [128, 128], BF16))
        identf_sb = ent(nc.sbuf_tensor("identf_sb", [128, 128], F32))
        onescolf_sb = ent(nc.sbuf_tensor("onescolf_sb", [128, 1], F32))
        cs_bc = ent(nc.sbuf_tensor("cs_bc", [128, 8], F32))
        ltri_sb = ent(nc.sbuf_tensor("ltri_sb", [128, 128], BF16))
        onescol_sb = ent(nc.sbuf_tensor("onescol_sb", [128, 1], BF16))
        ones1_sb = ent(nc.sbuf_tensor("ones1_sb", [1, 128], F32))
        iota8_sb = ent(nc.sbuf_tensor("iota8_sb", [128, 8], F32))
        iota8m_sb = ent(nc.sbuf_tensor("iota8m_sb", [128, 8], F32))
        expbase_sb = ent(nc.sbuf_tensor("expbase_sb", [128, 8], F32))
        iotap_sb = ent(nc.sbuf_tensor("iotap_sb", [128, 1], F32))
        iota127_sb = ent(nc.sbuf_tensor("iota127_sb", [128, 1], F32))
        dt_all = ent(nc.sbuf_tensor("dt_all", [128, H * 128], BF16))
        a_all = ent(nc.sbuf_tensor("a_all", [128, H], F32))
        kdec_all = ent(nc.sbuf_tensor("kdec_all", [128, H], F32))
        a128_all = ent(nc.sbuf_tensor("a128_all", [128, H], F32))
        lam_bc = ent(nc.sbuf_tensor("lam_bc", [128, H], F32))
        rg_sb = ent(nc.sbuf_tensor("rg_sb", [128, KH * 8], F32))
        nwrow_sb = ent(nc.sbuf_tensor("nwrow_sb", [1, HID], F32))
        m2_all = ent(nc.sbuf_tensor("m2_all", [128, N_TILES], F32))
        r2_all = ent(nc.sbuf_tensor("r2_all", [128, N_TILES], F32))
        s1_all = ent(nc.sbuf_tensor("s1_all", [128, N_TILES], F32))
        s2_all = ent(nc.sbuf_tensor("s2_all", [128, N_TILES], F32))
        w1s = ent(nc.sbuf_tensor("w1s", [128, N_TILES], F32))
        w2s = ent(nc.sbuf_tensor("w2s", [128, N_TILES], F32))
        d1u = ent(nc.sbuf_tensor("d1u", [128, N_TILES], U32))
        d2u = ent(nc.sbuf_tensor("d2u", [128, N_TILES], U32))
        base_sb = ent(nc.sbuf_tensor("base_sb", [1, 8], F32))
        eps5_sb = ent(nc.sbuf_tensor("eps5_sb", [128, 1], F32))
        eps24_sb = ent(nc.sbuf_tensor("eps24_sb", [128, 1], F32))

        with ExitStack() as isc:
            en = isc.enter_context
            mpos_c = en(nc.sbuf_tensor("mpos_c", [128, 128], F32))
            mask_c = en(nc.sbuf_tensor("mask_c", [128, 128], F32))
            alog_sb = en(nc.sbuf_tensor("alog_sb", [1, H], F32))
            lam1_sb = en(nc.sbuf_tensor("lam1_sb", [1, H], F32))
            t16a = en(nc.sbuf_tensor("t16a", [128, H], F32))
            t16b = en(nc.sbuf_tensor("t16b", [128, H], F32))
            t128a = en(nc.sbuf_tensor("t128a", [128, 128], F32))
            t128b = en(nc.sbuf_tensor("t128b", [128, 128], F32))
            ps_i = en(nc.psum_tensor("ps_init", [128, 512], F32))

            def emit_init(eng, e):
                c = EC(eng, e)
                for nm, t in (("ident", ident_sb), ("Ltri", ltri_sb), ("onescol", onescol_sb)):
                    c.dma(c.SY, lambda nm=nm, t=t: e.dma_start(
                        out=t.bitcast(mybir.dt.uint16)[:], in_=cd[nm].ap()), "dma_x")
                for nm, t in (("identf", identf_sb), ("onescolf", onescolf_sb),
                              ("ones1", ones1_sb), ("iota8", iota8_sb), ("iota8m", iota8m_sb),
                              ("expbase", expbase_sb), ("Mpos", mpos_c), ("mask", mask_c),
                              ("iotap", iotap_sb), ("iota127", iota127_sb)):
                    c.dma(c.SY, lambda nm=nm, t=t: e.dma_start(out=t[:], in_=cd[nm].ap()), "dma_x")
                c.dma(c.SY, lambda: e.dma_start(out=alog_sb[0:1, :], in_=alog[None, :]), "dma_x")
                for t in range(H):
                    c.dma(c.SY, lambda t=t: e.dma_start(
                        out=nwrow_sb[0:1, t * D:(t + 1) * D], in_=nw[None, :]), "dma_x")
                rgl = c.dma(c.SY, lambda: e.dma_start(
                    out=rg_sb[:].rearrange("p (k e) -> p k e", e=8),
                    in_=rgate.rearrange("(k p) e -> p k e", p=128)), "dma_x")
                ld = sems.val("dma_x")
                if c.AC:
                    c.w("dma_x", ld)
                c.a(lambda: nc.scalar.activation(lam1_sb[0:1, :], alog_sb[0:1, :], AF.Exp))
                aa = c.a(lambda: nc.scalar.mul(lam1_sb[0:1, :], lam1_sb[0:1, :], -1.0))
                if c.TE:
                    c.w("ac", aa)
                    c.w("dma_x", ld)
                tm = c.tmark(lambda: nc.tensor.matmul(ps_i[:, 0:16], ones1_sb[:],
                                                      lam1_sb[0:1, :], start=True, stop=True))
                if c.VE:
                    c.w("te", tm)
                c.v(lambda: nc.vector.tensor_copy(lam_bc[:], ps_i[:, 0:16]))
                c.v(lambda: nc.vector.tensor_tensor(
                    t16a[:], iotap_sb[:].broadcast_to([128, H]), lam_bc[:], ALU.mult))
                vv = c.v(lambda: nc.vector.tensor_tensor(
                    t16b[:], iota127_sb[:].broadcast_to([128, H]), lam_bc[:], ALU.mult))
                if c.AC:
                    c.w("ve", vv)
                c.a(lambda: nc.scalar.activation(a_all[:], t16a[:], AF.Exp))
                c.a(lambda: nc.scalar.activation(kdec_all[:], t16b[:], AF.Exp))
                c.a(lambda: nc.scalar.activation(a128_all[:], lam_bc[:], AF.Exp, scale=128.0))
                for h in range(H):
                    vv = c.v(lambda h=h: nc.vector.tensor_scalar(
                        t128a[:], mpos_c[:], lam_bc[:, h:h + 1], None, ALU.mult))
                    if c.AC:
                        c.w("ve", vv)
                    av = c.a(lambda: nc.scalar.activation(t128b[:], t128a[:], AF.Exp))
                    if c.VE:
                        c.w("ac", av)
                    c.v(lambda h=h: nc.vector.tensor_tensor(
                        dt_all[:, h * 128:(h + 1) * 128], t128b[:], mask_c[:], ALU.mult))
                c.v(lambda: nc.vector.memset(base_sb[0:1, :], 0.0))
                c.v(lambda: nc.vector.memset(eps5_sb[:], LN_EPS))
                c.v(lambda: nc.vector.memset(eps24_sb[:], L2_EPS))
                if c.TE:
                    c.w("dma_x", ld)
                    c.w("ve")
                for k in range(KH):
                    cmm = c.tmark(lambda k=k: nc.tensor.matmul(
                        ps_i[0:1, 32:40], onescolf_sb[:], rg_sb[:, k * 8:(k + 1) * 8],
                        start=(k == 0), stop=(k == KH - 1)))
                vv2 = c.v(lambda: nc.vector.tensor_copy(lam1_sb[0:1, 0:8], ps_i[0:1, 32:40])) if False else None
                if c.VE:
                    c.w("te", cmm)
                c.v(lambda: nc.vector.tensor_copy(alog_sb[0:1, 0:8], ps_i[0:1, 32:40]))
                cbv = sems.val("ve")
                if c.TE:
                    c.w("ve", cbv)
                bmm = c.tmark(lambda: nc.tensor.matmul(
                    ps_i[:, 64:72], ones1_sb[:], alog_sb[0:1, 0:8], start=True, stop=True))
                if c.VE:
                    c.w("te", bmm)
                c.v(lambda: nc.vector.tensor_copy(cs_bc[:], ps_i[:, 64:72]))

            with nc.Block("init") as blk:
                run_phase(blk, emit_init)

        def ffn_sublayer(tag, x_src, x_dst, stash_stats, first_call):
            with ExitStack() as fsc:
                fe = fsc.enter_context
                gate_res = fe(nc.sbuf_tensor(f"gres_{tag}", [128, KH * I], BF16))
                up_res = fe(nc.sbuf_tensor(f"ures_{tag}", [128, KH * I], BF16))
                cvt = [fe(nc.sbuf_tensor(f"cvt{i}_{tag}", [128, I // 4], BF16)) for i in range(2)]
                wconv = {}
                conv_all = []
                if first_call:
                    for _ex in range(E):
                        for _wsrc, _wdst, _nsl in ((gate_w, gate_bf, KH), (up_w, up_bf, KH),
                                                   (down_w, down_bf, MI)):
                            _cols = _wsrc.shape[2]
                            _nh = max(1, _cols // (I // 4))
                            for _k in range(_nsl):
                                for _hh in range(_nh):
                                    conv_all.append((_ex, _wsrc, _wdst, _k, _hh, _cols // _nh))

                def make_conv(c, e, sems, start_expert=0):
                    i0 = sum(1 for it_ in conv_all if it_[0] < start_expert)
                    state = {"i": i0, "loads": {}}

                    def emit_conv(n, upto_expert=None):
                        while state["i"] < len(conv_all):
                            if n <= 0:
                                return
                            i = state["i"]
                            ex, wsrc, wdst, k, hh, hc = conv_all[i]
                            if upto_expert is not None and ex > upto_expert:
                                return
                            pp = i % 2
                            if c.GP and (i - 2) in state["loads"]:
                                c.w("dma_g", state["loads"][i - 2][1])
                            lv = c.dma(c.GP, lambda ex=ex, wsrc=wsrc, k=k, pp=pp, hh=hh, hc=hc:
                                       e.dma_start(out=cvt[pp][:, 0:hc],
                                                   in_=wsrc[ex, k * 128:(k + 1) * 128,
                                                            hh * hc:(hh + 1) * hc]), "dma_g")
                            if c.GP:
                                c.w("dma_g", lv)
                            sv_ = c.dma(c.GP, lambda ex=ex, wdst=wdst, k=k, pp=pp, hh=hh, hc=hc:
                                        e.dma_start(out=wdst[ex, k * 128:(k + 1) * 128,
                                                             hh * hc:(hh + 1) * hc],
                                                    in_=cvt[pp][:, 0:hc]), "dma_g")
                            state["loads"][i] = (lv, sv_)
                            if i == len(conv_all) - 1 or conv_all[i + 1][0] != ex:
                                wconv[ex] = sv_
                            state["i"] += 1
                            n -= 1
                    return emit_conv

                with ExitStack() as asc:
                    an = asc.enter_context
                    xa = [an(nc.sbuf_tensor(f"xa{i}_{tag}", [128, HID], F32)) for i in range(2)]
                    xln = [an(nc.sbuf_tensor(f"xln{i}_{tag}", [128, HID], BF16)) for i in range(2)]
                    xaT = an(nc.sbuf_tensor(f"xaT_{tag}", [128, HID], F32))
                    sqt = an(nc.sbuf_tensor(f"sqt_{tag}", [128, HID], F32))
                    r8a = an(nc.sbuf_tensor(f"r8a_{tag}", [128, 8], F32))
                    r8b = an(nc.sbuf_tensor(f"r8b_{tag}", [128, 8], F32))
                    r8c = an(nc.sbuf_tensor(f"r8c_{tag}", [128, 8], F32))
                    oh1 = an(nc.sbuf_tensor(f"oh1_{tag}", [128, 8], F32))
                    oh2 = an(nc.sbuf_tensor(f"oh2_{tag}", [128, 8], F32))
                    ohb = an(nc.sbuf_tensor(f"ohb_{tag}", [128, 8], BF16))
                    slg = an(nc.sbuf_tensor(f"slg_{tag}", [128, 8], F32))
                    bvec = an(nc.sbuf_tensor(f"bvec_{tag}", [128, 8], F32))
                    cc = [an(nc.sbuf_tensor(f"cc{i}_{tag}", [128, 1], F32)) for i in range(8)]
                    psXT = an(nc.psum_tensor(f"psXT_{tag}", [128, 2048], F32))
                    psA = an(nc.psum_tensor(f"psA_{tag}", [128, 1024], F32))
                    ps8 = an(nc.psum_tensor(f"ps8_{tag}", [128, 1024], F32))

                    def emit_A(eng, e):
                        c = EC(eng, e)
                        c.barrier()
                        bc_reg = e.to_reg(E * C_LOC - 1) if c.GP else None
                        emit_conv = make_conv(c, e, sems)
                        scat_done = {}
                        evacT = {}
                        rmax_ve = {}
                        rexp_ac = {}
                        pdone = {}
                        for t in range(N_TILES):
                            par = t % 2
                            if c.SY and t >= 2:
                                c.w("dma_g", scat_done[t - 2])
                            ldv = c.dma(c.SY, lambda: e.dma_start(
                                out=xa[par][:], in_=x_src[t * 128:(t + 1) * 128, :]), "dma_x")
                            if c.VE:
                                c.w("dma_x", ldv)
                            c.v(lambda: nc.vector.reduce_sum(cc[0][:], xa[par][:], axis=AX.X))
                            c.v(lambda: nc.vector.tensor_tensor(sqt[:], xa[par][:], xa[par][:], ALU.mult))
                            c.v(lambda: nc.vector.reduce_sum(cc[1][:], sqt[:], axis=AX.X))
                            c.v(lambda: nc.vector.tensor_scalar(cc[2][:], cc[0][:], 1.0 / HID, None, ALU.mult))
                            c.v(lambda: nc.vector.tensor_tensor(cc[3][:], cc[2][:], cc[2][:], ALU.mult))
                            c.v(lambda: nc.vector.tensor_scalar(cc[4][:], cc[1][:], 1.0 / HID, None, ALU.mult))
                            vst = c.v(lambda: nc.vector.tensor_sub(cc[4][:], cc[4][:], cc[3][:]))
                            if c.AC:
                                c.w("ve", vst)
                            c.a(lambda: nc.scalar.activation(cc[5][:], cc[4][:], AF.Ln, bias=eps5_sb[:]))
                            acv = c.a(lambda: nc.scalar.activation(cc[5][:], cc[5][:], AF.Exp, scale=-0.5))
                            if c.VE:
                                c.w("ac", acv)
                                if t >= 2:
                                    c.w("dma_g", scat_done[t - 2])
                            lnv = c.v(lambda: nc.vector.tensor_scalar(
                                xln[par][:], xa[par][:], cc[2][:], cc[5][:], ALU.subtract, ALU.mult))
                            if c.TE:
                                c.w("dma_x", ldv)
                                if t >= 2:
                                    c.w("ve", evacT[t - 2])
                            for f in range(KH):
                                ti = c.t(lambda f=f: nc.tensor.transpose(
                                    psXT[:, par * 1024 + f * 128: par * 1024 + (f + 1) * 128],
                                    xa[par][:, f * 128:(f + 1) * 128], identf_sb[:]),
                                    inc=(f == KH - 1))
                            ti = sems.val("te")
                            if c.VE:
                                c.w("te", ti)
                            evacT[t] = c.v(lambda: nc.vector.tensor_copy(
                                xaT[:], psXT[:, par * 1024: par * 1024 + 1024]))
                            if c.TE:
                                c.w("ve", evacT[t])
                                if t >= 2:
                                    c.w("ve", rmax_ve[t - 2])
                                    c.w("ac", rexp_ac[t - 2])
                            for k in range(KH):
                                c.t(lambda k=k: nc.tensor.matmul(
                                    psA[:, par * 512: par * 512 + 8],
                                    xaT[:, k * 128:(k + 1) * 128], rg_sb[:, k * 8:(k + 1) * 8],
                                    start=(k == 0), stop=(k == KH - 1)), inc=(k == KH - 1))
                            rmm = sems.val("te")
                            rawlog = psA[:, par * 512: par * 512 + 8]
                            if c.VE:
                                c.w("te", rmm)
                            c.v(lambda: nc.vector.tensor_scalar(r8b[:], cs_bc[:], cc[2][:], None, ALU.mult))
                            c.v(lambda: nc.vector.tensor_tensor(r8a[:], rawlog, r8b[:], ALU.subtract))
                            c.v(lambda: nc.vector.tensor_scalar(r8a[:], r8a[:], cc[5][:], None, ALU.mult))
                            c.v(lambda: nc.vector.reduce_max(cc[6][:], r8a[:], axis=AX.X))
                            rmax_ve[t] = c.v(lambda: nc.vector.tensor_scalar(
                                cc[6][:], cc[6][:], -1.0, None, ALU.mult))
                            if c.AC:
                                c.w("ve", rmax_ve[t])
                            rexp_ac[t] = c.a(lambda: nc.scalar.activation(
                                r8a[:], r8a[:], AF.Exp, bias=cc[6][:]))
                            if c.VE:
                                c.w("ac", rexp_ac[t])
                            c.v(lambda: nc.vector.reduce_sum(cc[7][:], r8a[:], axis=AX.X))
                            c.v(lambda: nc.vector.reciprocal(cc[7][:], cc[7][:]))
                            c.v(lambda: nc.vector.tensor_scalar(r8a[:], r8a[:], cc[7][:], None, ALU.mult))
                            c.v(lambda: nc.vector.reduce_max(cc[0][:], r8a[:], axis=AX.X))
                            c.v(lambda: nc.vector.tensor_scalar(r8b[:], r8a[:], cc[0][:], None, ALU.subtract))
                            c.v(lambda: nc.vector.tensor_scalar(r8b[:], r8b[:], 0.0, None, ALU.is_ge))
                            c.v(lambda: nc.vector.tensor_tensor(r8c[:], r8b[:], iota8m_sb[:], ALU.mult))
                            c.v(lambda: nc.vector.tensor_scalar(r8c[:], r8c[:], 99.0, None, ALU.add))
                            c.v(lambda: nc.vector.tensor_reduce(cc[1][:], r8c[:], axis=AX.X, op=ALU.min))
                            c.v(lambda: nc.vector.tensor_scalar(oh1[:], iota8_sb[:], cc[1][:], None, ALU.subtract))
                            c.v(lambda: nc.vector.tensor_scalar(oh1[:], oh1[:], 0.0, None, ALU.is_equal))
                            c.v(lambda: nc.vector.scalar_tensor_tensor(r8b[:], oh1[:], -1e9, r8a[:], ALU.mult, ALU.add))
                            c.v(lambda: nc.vector.reduce_max(cc[2][:], r8b[:], axis=AX.X))
                            c.v(lambda: nc.vector.tensor_scalar(r8c[:], r8b[:], cc[2][:], None, ALU.subtract))
                            c.v(lambda: nc.vector.tensor_scalar(r8c[:], r8c[:], 0.0, None, ALU.is_ge))
                            c.v(lambda: nc.vector.tensor_tensor(r8c[:], r8c[:], iota8m_sb[:], ALU.mult))
                            c.v(lambda: nc.vector.tensor_scalar(r8c[:], r8c[:], 99.0, None, ALU.add))
                            c.v(lambda: nc.vector.tensor_reduce(cc[3][:], r8c[:], axis=AX.X, op=ALU.min))
                            c.v(lambda: nc.vector.tensor_scalar(oh2[:], iota8_sb[:], cc[3][:], None, ALU.subtract))
                            c.v(lambda: nc.vector.tensor_scalar(oh2[:], oh2[:], 0.0, None, ALU.is_equal))
                            c.v(lambda: nc.vector.tensor_tensor(cc[4][:], cc[0][:], cc[2][:], ALU.add))
                            c.v(lambda: nc.vector.tensor_scalar(cc[4][:], cc[4][:], 1e-9, None, ALU.add))
                            c.v(lambda: nc.vector.reciprocal(cc[4][:], cc[4][:]))
                            vohb = c.v(lambda: nc.vector.tensor_tensor(ohb[:], oh1[:], oh2[:], ALU.add))
                            if c.TE:
                                c.w("ve", vohb)
                                if t >= 2:
                                    c.w("ve", pdone[t - 2])
                            c.t(lambda: nc.tensor.matmul(
                                ps8[:, par * 512: par * 512 + 8], ltri_sb[:], ohb[:],
                                start=True, stop=True))
                            c.t(lambda: nc.tensor.matmul(
                                ps8[0:1, par * 512 + 16: par * 512 + 24], onescol_sb[:], ohb[:],
                                start=True, stop=True))
                            pmm = c.tmark(lambda: nc.tensor.matmul(
                                ps8[:, par * 512 + 32: par * 512 + 40], ones1_sb[:],
                                base_sb[0:1, :], start=True, stop=True))
                            if c.VE:
                                c.w("te", pmm)
                            c.v(lambda: nc.vector.tensor_copy(bvec[:], ps8[:, par * 512 + 32: par * 512 + 40]))
                            c.v(lambda: nc.vector.tensor_tensor(
                                slg[:], ps8[:, par * 512: par * 512 + 8], bvec[:], ALU.add))
                            pdone[t] = c.v(lambda: nc.vector.tensor_tensor(
                                base_sb[0:1, :], base_sb[0:1, :],
                                ps8[0:1, par * 512 + 16: par * 512 + 24], ALU.add))
                            c.v(lambda: nc.vector.tensor_tensor(r8b[:], slg[:], expbase_sb[:], ALU.add))
                            c.v(lambda: nc.vector.tensor_scalar(r8c[:], slg[:], float(C_LOC), None, ALU.is_ge))
                            c.v(lambda: nc.vector.scalar_tensor_tensor(r8b[:], r8c[:], 1e6, r8b[:], ALU.mult, ALU.add))
                            c.v(lambda: nc.vector.tensor_tensor(r8a[:], r8b[:], oh1[:], ALU.mult))
                            c.v(lambda: nc.vector.reduce_sum(cc[5][:], r8a[:], axis=AX.X))
                            c.v(lambda: nc.vector.tensor_copy(d1u[:, t:t + 1], cc[5][:]))
                            c.v(lambda: nc.vector.tensor_tensor(r8a[:], r8b[:], oh2[:], ALU.mult))
                            c.v(lambda: nc.vector.reduce_sum(cc[6][:], r8a[:], axis=AX.X))
                            c.v(lambda: nc.vector.tensor_copy(d2u[:, t:t + 1], cc[6][:]))
                            c.v(lambda: nc.vector.tensor_tensor(r8a[:], r8c[:], oh1[:], ALU.mult))
                            c.v(lambda: nc.vector.reduce_sum(cc[5][:], r8a[:], axis=AX.X))
                            c.v(lambda: nc.vector.tensor_scalar(cc[5][:], cc[5][:], -1.0, 1.0, ALU.mult, ALU.add))
                            c.v(lambda: nc.vector.tensor_tensor(cc[5][:], cc[5][:], cc[4][:], ALU.mult))
                            c.v(lambda: nc.vector.tensor_tensor(cc[5][:], cc[5][:], cc[0][:], ALU.mult))
                            c.v(lambda: nc.vector.tensor_copy(w1s[:, t:t + 1], cc[5][:]))
                            c.v(lambda: nc.vector.tensor_tensor(r8a[:], r8c[:], oh2[:], ALU.mult))
                            c.v(lambda: nc.vector.reduce_sum(cc[6][:], r8a[:], axis=AX.X))
                            c.v(lambda: nc.vector.tensor_scalar(cc[6][:], cc[6][:], -1.0, 1.0, ALU.mult, ALU.add))
                            c.v(lambda: nc.vector.tensor_tensor(cc[6][:], cc[6][:], cc[4][:], ALU.mult))
                            c.v(lambda: nc.vector.tensor_tensor(cc[6][:], cc[6][:], cc[2][:], ALU.mult))
                            vdw = c.v(lambda: nc.vector.tensor_copy(w2s[:, t:t + 1], cc[6][:]))
                            if c.GP:
                                c.w("ve", vdw)
                            c.dma(c.GP, lambda: e.indirect_dma_start(
                                out=xe_d, out_offset=IndirectOffsetOnAxis(d1u[:, t:t + 1], 0),
                                in_=xln[par][:], in_offset=None,
                                bounds_check=bc_reg, oob_is_err=False), "dma_g")
                            scat_done[t] = c.dma(c.GP, lambda: e.indirect_dma_start(
                                out=xe_d, out_offset=IndirectOffsetOnAxis(d2u[:, t:t + 1], 0),
                                in_=xln[par][:], in_offset=None,
                                bounds_check=bc_reg, oob_is_err=False), "dma_g")
                            emit_conv(2, upto_expert=0)
                        emit_conv(10**6, upto_expert=0)
                        c.v(lambda: nc.vector.memset(base_sb[0:1, :], 0.0))

                    with nc.Block(f"ffnA_{tag}") as blk:
                        run_phase(blk, emit_A)

                with ExitStack() as bsc:
                    bn = bsc.enter_context
                    xet = [bn(nc.sbuf_tensor(f"xet{i}_{tag}", [128, KH * ST], BF16)) for i in range(2)]
                    hT = bn(nc.sbuf_tensor(f"hT_{tag}", [128, MI * ST], BF16))
                    dwn = [bn(nc.sbuf_tensor(f"dwn{i}_{tag}", [128, HID], BF16)) for i in range(4)]
                    tsil = [bn(nc.sbuf_tensor(f"tsl{i}_{tag}", [128, ST], BF16)) for i in range(2)]
                    yesb = [bn(nc.sbuf_tensor(f"yes{i}_{tag}", [128, HID], BF16)) for i in range(2)]
                    psB = bn(nc.psum_tensor(f"psB_{tag}", [128, 4096], F32))
                    dbg_ps_sb = bn(nc.sbuf_tensor(f"dbgps_{tag}", [128, 3072], F32)) if tap == "dbg1" else None

                    def bank(b, n=512):
                        return psB[:, b * 512: b * 512 + n]

                    def emit_B(eng, e):
                        c = EC(eng, e)
                        c.barrier()
                        emit_conv = make_conv(c, e, sems, start_expert=1)
                        scat_target = sems.val("dma_g")
                        evv = {}
                        gu_done = {}
                        xet_loads = {}
                        dmm_prev = None
                        ye_ev = None
                        ye_last_store = {}

                        def issue_xet(j):
                            if j >= E * N_ST:
                                return
                            ex_, st_ = divmod(j, N_ST)
                            stn_ = ST_SIZES[st_]
                            s0_ = ex_ * C_LOC + st_ * ST
                            par_ = j % 2
                            if c.AC or c.SY:
                                c.w("dma_g", scat_target)
                                if (j - 2) in gu_done:
                                    c.w("te", gu_done[j - 2])
                            for f in range(KH):
                                xl_ = c.dma(c.AC, lambda f=f, stn_=stn_, s0_=s0_, par_=par_:
                                            e.dma_start(
                                                out=xet[par_][:, f * ST: f * ST + stn_],
                                                in_=xe_d[s0_: s0_ + stn_, f * 128:(f + 1) * 128],
                                                transpose=True), "dma_s")
                            xet_loads[j] = (sems.val("dma_s"), sems.val("dma_t"))

                        issue_xet(0)
                        for ex in range(E):
                            if gu_done:
                                last_j = max(gu_done)
                                if c.SY or c.GP:
                                    c.w("te", gu_done[last_j])
                            if first_call and (c.SY or c.GP):
                                c.w("dma_g", wconv[ex])
                            for res, src in ((gate_res, gate_bf), (up_res, up_bf)):
                                for k in range(KH):
                                    act, semn = (c.SY, "dma_w") if k % 2 == 0 else (c.GP, "dma_v")
                                    c.dma(act, lambda k=k, res=res, src=src: e.dma_start(
                                        out=res[:, k * I:(k + 1) * I],
                                        in_=src[ex, k * 128:(k + 1) * 128, :]), semn)
                            wl_s = sems.val("dma_w")
                            wl_v = sems.val("dma_v")
                            if first_call and ex + 1 < E:
                                emit_conv(10**6, upto_expert=ex + 1)

                            for st in range(N_ST):
                                j = ex * N_ST + st
                                stn = ST_SIZES[st]
                                smn = stn // 128
                                s0 = ex * C_LOC + st * ST
                                par = j % 2
                                if c.TE:
                                    c.w("dma_s", xet_loads[j][0])
                                    c.w("dma_t", xet_loads[j][1])
                                    c.w("dma_w", wl_s)
                                    c.w("dma_v", wl_v)
                                    if ye_ev is not None:
                                        c.w("ve", ye_ev)
                                for m in range(MI):
                                    gb, ub = (0, 1) if m % 2 == 0 else (2, 3)
                                    if c.TE and m >= 2:
                                        c.w("ve", evv[m - 2])
                                    for k in range(KH):
                                        c.t(lambda k=k, m=m, gb=gb, stn=stn, par=par: nc.tensor.matmul(
                                            bank(gb, stn),
                                            gate_res[:, k * I + m * 128: k * I + (m + 1) * 128],
                                            xet[par][:, k * ST: k * ST + stn],
                                            start=(k == 0), stop=(k == KH - 1)),
                                            inc=(k == KH - 1))
                                    gmm = sems.val("te")
                                    for k in range(KH):
                                        c.t(lambda k=k, m=m, ub=ub, stn=stn, par=par: nc.tensor.matmul(
                                            bank(ub, stn),
                                            up_res[:, k * I + m * 128: k * I + (m + 1) * 128],
                                            xet[par][:, k * ST: k * ST + stn],
                                            start=(k == 0), stop=(k == KH - 1)),
                                            inc=(k == KH - 1))
                                    umm = sems.val("te")
                                    if c.AC:
                                        c.w("te", gmm)
                                        if m >= 2:
                                            c.w("ve", evv[m - 2])
                                    asil = c.a(lambda m=m, gb=gb, stn=stn: nc.scalar.activation(
                                        tsil[m % 2][:, 0:stn], bank(gb, stn), AF.Silu))
                                    if c.VE:
                                        c.w("te", umm)
                                        c.w("ac", asil)
                                        if m < 2 and dmm_prev is not None:
                                            c.w("te", dmm_prev)
                                    evv[m] = c.v(lambda m=m, ub=ub, stn=stn: nc.vector.tensor_tensor(
                                        hT[:, m * ST: m * ST + stn], tsil[m % 2][:, 0:stn],
                                        bank(ub, stn), ALU.mult))
                                gu_done[j] = sems.val("te")
                                issue_xet(j + 1)
                                dl = {}
                                for k2 in range(MI):
                                    dpar = k2 % 4
                                    dsplit = (k2 % 2 == 0) or first_call
                                    act, semn = (c.SY, "dma_w") if dsplit else (c.GP, "dma_v")
                                    if (c.SY and dsplit) or (c.GP and not dsplit):
                                        if k2 < 4 and dmm_prev is not None:
                                            c.w("te", dmm_prev)
                                        if k2 >= 4:
                                            c.w("te", dl[k2 - 4][1])
                                    dv = c.dma(act, lambda k2=k2, dpar=dpar: e.dma_start(
                                        out=dwn[dpar][:],
                                        in_=down_bf[ex, k2 * 128:(k2 + 1) * 128, :]), semn)
                                    if c.TE:
                                        c.w(semn, dv)
                                        if k2 == 0:
                                            c.w("ve", evv[MI - 2])
                                            c.w("ve", evv[MI - 1])
                                    for sM in range(smn):
                                        for n2 in range(2):
                                            bk = sM * 2 + n2
                                            c.t(lambda k2=k2, sM=sM, n2=n2, bk=bk, dpar=dpar: nc.tensor.matmul(
                                                bank(bk),
                                                hT[:, k2 * ST + sM * 128: k2 * ST + sM * 128 + 128],
                                                dwn[dpar][:, n2 * 512:(n2 + 1) * 512],
                                                start=(k2 == 0), stop=(k2 == MI - 1)),
                                                inc=(sM == smn - 1 and n2 == 1))
                                    dl[k2] = (dv, sems.val("te"))
                                dmm_prev = sems.val("te")
                                if tap == "dbg1" and ex == DBG_EX and st == DBG_ST:
                                    if c.VE:
                                        c.w("te", dmm_prev)
                                    for bki in range(6):
                                        dpv = c.v(lambda bki=bki: nc.vector.tensor_copy(
                                            dbg_ps_sb[:, bki * 512:(bki + 1) * 512], bank(bki)))
                                    if c.AC:
                                        c.w("ve", dpv)
                                    c.dma(c.AC, lambda: e.dma_start(out=dbg_ps[:], in_=dbg_ps_sb[:]), "st")
                                for sM in range(smn):
                                    ypar = sM % 2
                                    if c.VE:
                                        c.w("te", dmm_prev)
                                        if ypar in ye_last_store:
                                            c.w("dma_s", ye_last_store[ypar])
                                    for n2 in range(2):
                                        yev = c.v(lambda sM=sM, n2=n2, ypar=ypar: nc.vector.tensor_copy(
                                            yesb[ypar][:, n2 * 512:(n2 + 1) * 512],
                                            bank(sM * 2 + n2)))
                                    if c.AC:
                                        c.w("ve", yev)
                                    ye_last_store[ypar] = c.dma(c.AC, lambda sM=sM, ypar=ypar: e.dma_start(
                                        out=ye_d[s0 + sM * 128: s0 + (sM + 1) * 128, :],
                                        in_=yesb[ypar][:]), "dma_s")
                                ye_ev = sems.val("ve")

                    with nc.Block(f"ffnB_{tag}") as blk:
                        run_phase(blk, emit_B)

                with ExitStack() as csc:
                    cn = csc.enter_context
                    xc = [cn(nc.sbuf_tensor(f"xc{i}_{tag}", [128, HID], F32)) for i in range(2)]
                    y1t = [cn(nc.sbuf_tensor(f"y1t{i}_{tag}", [128, HID], BF16)) for i in range(2)]
                    y2t = [cn(nc.sbuf_tensor(f"y2t{i}_{tag}", [128, HID], BF16)) for i in range(2)]
                    comb = [cn(nc.sbuf_tensor(f"comb{i}_{tag}", [128, HID], F32)) for i in range(2)]
                    sqc = cn(nc.sbuf_tensor(f"sqc_{tag}", [128, HID], F32))

                    def emit_C(eng, e):
                        c = EC(eng, e)
                        c.barrier()
                        bc_reg = e.to_reg(E * C_LOC - 1) if c.GP else None
                        cdone = {}
                        stdone = {}
                        for t in range(N_TILES):
                            par = t % 2
                            if c.SY and t >= 2:
                                c.w("ve", cdone[t - 2])
                            ldv = c.dma(c.SY, lambda: e.dma_start(
                                out=xc[par][:], in_=x_src[t * 128:(t + 1) * 128, :]), "dma_x")
                            if c.GP and t >= 2:
                                c.w("ve", cdone[t - 2])
                            c.dma(c.GP, lambda: e.indirect_dma_start(
                                out=y1t[par][:], out_offset=None,
                                in_=ye_d, in_offset=IndirectOffsetOnAxis(d1u[:, t:t + 1], 0),
                                bounds_check=bc_reg, oob_is_err=False), "dma_g")
                            gl = c.dma(c.GP, lambda: e.indirect_dma_start(
                                out=y2t[par][:], out_offset=None,
                                in_=ye_d, in_offset=IndirectOffsetOnAxis(d2u[:, t:t + 1], 0),
                                bounds_check=bc_reg, oob_is_err=False), "dma_g")
                            if c.VE:
                                c.w("dma_x", ldv)
                                c.w("dma_g", gl)
                                if t >= 2:
                                    c.w("st", stdone[t - 2])
                            c.v(lambda: nc.vector.tensor_scalar(
                                comb[par][:], y1t[par][:], w1s[:, t:t + 1], None, ALU.mult))
                            c.v(lambda: nc.vector.scalar_tensor_tensor(
                                comb[par][:], y2t[par][:], w2s[:, t:t + 1], comb[par][:], ALU.mult, ALU.add))
                            cv = c.v(lambda: nc.vector.scalar_tensor_tensor(
                                comb[par][:], comb[par][:], 0.5, xc[par][:], ALU.mult, ALU.add))
                            if stash_stats:
                                c.v(lambda: nc.vector.reduce_sum(s1_all[:, t:t + 1], comb[par][:], axis=AX.X))
                                c.v(lambda: nc.vector.tensor_tensor(sqc[:], comb[par][:], comb[par][:], ALU.mult))
                                cv = c.v(lambda: nc.vector.reduce_sum(s2_all[:, t:t + 1], sqc[:], axis=AX.X))
                            cdone[t] = cv
                            if c.AC:
                                c.w("ve", cv)
                            stdone[t] = c.dma(c.AC, lambda par=par: e.dma_start(
                                out=x_dst[t * 128:(t + 1) * 128, :], in_=comb[par][:]), "st")
                        if stash_stats:
                            c.v(lambda: nc.vector.tensor_scalar(m2_all[:], s1_all[:], 1.0 / HID, None, ALU.mult))
                            c.v(lambda: nc.vector.tensor_tensor(sqc[:, 0:N_TILES], m2_all[:], m2_all[:], ALU.mult))
                            c.v(lambda: nc.vector.tensor_scalar(r2_all[:], s2_all[:], 1.0 / HID, None, ALU.mult))
                            sv = c.v(lambda: nc.vector.tensor_sub(r2_all[:], r2_all[:], sqc[:, 0:N_TILES]))
                            if c.AC:
                                c.w("ve", sv)
                            c.a(lambda: nc.scalar.activation(r2_all[:], r2_all[:], AF.Ln, bias=eps5_sb[:]))
                            c.a(lambda: nc.scalar.activation(r2_all[:], r2_all[:], AF.Exp, scale=-0.5))

                    with nc.Block(f"ffnC_{tag}") as blk:
                        run_phase(blk, emit_C)

        def attn_sublayer():
            with ExitStack() as atc:
                at = atc.enter_context
                wo_res = at(nc.sbuf_tensor("wo_res", [128, KH * HID], BF16))
                with ExitStack() as psc:
                    pn = psc.enter_context
                    wq_res = pn(nc.sbuf_tensor("wq_res", [128, KH * HID], BF16))
                    wk_res = pn(nc.sbuf_tensor("wk_res", [128, KH * HID], BF16))
                    wv_res = pn(nc.sbuf_tensor("wv_res", [128, KH * HID], BF16))
                    wf_res = pn(nc.sbuf_tensor("wf_res", [128, KH * HID], BF16))
                    wg_res = pn(nc.sbuf_tensor("wg_res", [128, KH * HID], BF16))
                    wb_res = pn(nc.sbuf_tensor("wb_res", [128, KH * H], BF16))
                    dtb_bc = pn(nc.sbuf_tensor("dtb_bc", [128, HID], F32))
                    bg_bc = pn(nc.sbuf_tensor("bg_bc", [128, HID], F32))
                    nw_bc = pn(nc.sbuf_tensor("nw_bc", [128, HID], F32))
                    row_sb = pn(nc.sbuf_tensor("row_sb", [1, HID], F32))
                    xa2 = [pn(nc.sbuf_tensor(f"xa2{i}", [128, HID], F32)) for i in range(2)]
                    xln2 = pn(nc.sbuf_tensor("xln2", [128, HID], BF16))
                    lnT2 = pn(nc.sbuf_tensor("lnT2", [128, HID], BF16))
                    qt_s = [pn(nc.sbuf_tensor("qt_s{}".format(i), [128, HID], BF16)) for i in range(2)]
                    kt_s = [pn(nc.sbuf_tensor("kt_s{}".format(i), [128, HID], BF16)) for i in range(2)]
                    vt_s = [pn(nc.sbuf_tensor("vt_s{}".format(i), [128, HID], BF16)) for i in range(2)]
                    qst_s = [pn(nc.sbuf_tensor("qst_s{}".format(i), [128, HID], BF16)) for i in range(2)]
                    gt_s = [pn(nc.sbuf_tensor("gt_s{}".format(i), [128, HID], BF16)) for i in range(2)]
                    g2t_s = [pn(nc.sbuf_tensor("g2t_s{}".format(i), [128, HID], BF16)) for i in range(2)]
                    sig_s = pn(nc.sbuf_tensor("sig_s", [128, HID], F32))
                    sq2_s = pn(nc.sbuf_tensor("sq2_s", [128, HID], F32))
                    trev = [pn(nc.sbuf_tensor(f"trev{i}", [128, HID], BF16)) for i in range(2)]
                    aux_s = [pn(nc.sbuf_tensor("aux_s{}".format(i), [128, 64], F32)) for i in range(2)]
                    psP = pn(nc.psum_tensor("psP", [128, 3072], F32))
                    psPT = pn(nc.psum_tensor("psPT", [128, 2048], BF16))

                    def emit_P(eng, e):
                        c = EC(eng, e)
                        c.barrier()
                        wl = {}
                        for nm, res, src, width in (("q", wq_res, wq, HID), ("k", wk_res, wk, HID),
                                                    ("v", wv_res, wv, HID), ("b", wb_res, wb_in, H),
                                                    ("f", wf_res, wf, HID), ("g", wg_res, wg, HID),
                                                    ("o", wo_res, wo, HID)):
                            for k in range(KH):
                                wl[nm] = c.dma(c.GP, lambda nm=nm, res=res, src=src, width=width, k=k:
                                               e.dma_start(out=res[:, k * width:(k + 1) * width],
                                                           in_=src[k * 128:(k + 1) * 128, :]), "dma_g")
                        bdone = {}
                        for bi, (srcrow, dst) in enumerate(((dtb, dtb_bc), (bgv, bg_bc), (None, nw_bc))):
                            if srcrow is not None:
                                if c.SY and bi > 0:
                                    c.w("ve", bdone[bi - 1])
                                rv = c.dma(c.SY, lambda srcrow=srcrow: e.dma_start(
                                    out=row_sb[0:1, :], in_=srcrow[None, :]), "dma_x")
                                rsrc = row_sb
                            else:
                                rv = None
                                rsrc = nwrow_sb
                            if c.TE:
                                if rv is not None:
                                    c.w("dma_x", rv)
                                if bi > 0:
                                    c.w("ve", bdone[bi - 1])
                            for n2 in range(2):
                                tv = c.tmark(lambda n2=n2, rsrc=rsrc: nc.tensor.matmul(
                                    psP[:, n2 * 512:(n2 + 1) * 512], ones1_sb[:],
                                    rsrc[0:1, n2 * 512:(n2 + 1) * 512], start=True, stop=True))
                            if c.VE:
                                c.w("te", tv)
                            bdone[bi] = c.v(lambda dst=dst: nc.vector.tensor_copy(dst[:], psP[:, 0:1024]))
                        bc_done = bdone[2]

                        st_done = {}
                        trev_st = {}
                        tg_evac = {}
                        tg_count = 0
                        lnvs = {}
                        ldvs = {}
                        btas = {}
                        gvs = {}

                        def issue_x1(t2):
                            if t2 >= N_TILES:
                                return
                            if c.SY and t2 >= 2:
                                c.w("ve", lnvs[t2 - 2])
                            ldvs[t2] = c.dma(c.SY, lambda t2=t2: e.dma_start(
                                out=xa2[t2 % 2][:], in_=x1_d[t2 * 128:(t2 + 1) * 128, :]), "dma_x")

                        issue_x1(0)
                        for t in range(N_TILES):
                            par = t % 2
                            issue_x1(t + 1)
                            ldv = ldvs[t]
                            if c.VE:
                                c.w("dma_x", ldv)
                                if t >= 2:
                                    c.w("dma_x", st_done[t - 2])
                                    if (t % 2) in trev_st:
                                        c.w("dma_s", trev_st[t % 2])
                            lnv = c.v(lambda: nc.vector.tensor_scalar(
                                xln2[:], xa2[par][:], m2_all[:, t:t + 1], r2_all[:, t:t + 1],
                                ALU.subtract, ALU.mult))
                            lnvs[t] = lnv
                            rotp = tg_count % 2
                            if c.TE:
                                c.w("ve", lnv)
                                if tg_count >= 2:
                                    c.w("ve", tg_evac[tg_count - 2])
                            for f in range(KH):
                                c.t(lambda f=f, rotp=rotp: nc.tensor.transpose(
                                    psPT[:, rotp * 1024 + f * 128: rotp * 1024 + (f + 1) * 128],
                                    xln2[:, f * 128:(f + 1) * 128], ident_sb[:]),
                                    inc=(f == KH - 1))
                            tv = sems.val("te")
                            if c.VE:
                                c.w("te", tv)
                            lt = c.v(lambda rotp=rotp: nc.vector.tensor_copy(
                                lnT2[:], psPT[:, rotp * 1024: rotp * 1024 + 1024]))
                            tg_evac[tg_count] = lt
                            tg_count += 1
                            if c.TE:
                                c.w("ve", lt)
                                c.w("dma_g", wl["v"])
                                if t >= 1:
                                    c.w("ve", gvs[t - 1])
                                    c.w("ac", btas[t - 1])
                            sw1p = {}
                            for k in range(KH):
                                for pi, res in ((0, wq_res), (1, wk_res), (2, wv_res)):
                                    for n2 in range(2):
                                        c.t(lambda k=k, pi=pi, res=res, n2=n2: nc.tensor.matmul(
                                            psP[:, (pi * 2 + n2) * 512:(pi * 2 + n2) * 512 + 512],
                                            lnT2[:, k * 128:(k + 1) * 128],
                                            res[:, k * HID + n2 * 512: k * HID + (n2 + 1) * 512],
                                            start=(k == 0), stop=(k == KH - 1)),
                                            inc=(k == KH - 1))
                                        if k == KH - 1:
                                            sw1p[pi * 2 + n2] = sems.val("te")
                            sw1 = sems.val("te")
                            evm = {}
                            for pi, dst in ((0, qt_s[par]), (1, kt_s[par]), (2, vt_s[par])):
                                for n2 in range(2):
                                    bk = pi * 2 + n2
                                    if c.AC:
                                        c.w("te", sw1p[bk])
                                        if bk >= 2:
                                            c.w("ve", evm[bk - 2])
                                    sg = c.a(lambda bk=bk, n2=n2: nc.scalar.activation(
                                        sig_s[:, n2 * 512:(n2 + 1) * 512],
                                        psP[:, bk * 512: bk * 512 + 512], AF.Sigmoid))
                                    if c.VE:
                                        c.w("ac", sg)
                                    evm[bk] = c.v(lambda dst=dst, bk=bk, n2=n2: nc.vector.tensor_tensor(
                                        dst[:, n2 * 512:(n2 + 1) * 512],
                                        psP[:, bk * 512: bk * 512 + 512],
                                        sig_s[:, n2 * 512:(n2 + 1) * 512], ALU.mult))
                            qkv_ev = sems.val("ve")
                            if c.TE:
                                c.w("ve", qkv_ev)
                                c.w("dma_g", wl["g"])
                            for k in range(KH):
                                for pi, res in ((0, wf_res), (1, wg_res)):
                                    for n2 in range(2):
                                        c.t(lambda k=k, pi=pi, res=res, n2=n2: nc.tensor.matmul(
                                            psP[:, (pi * 2 + n2) * 512:(pi * 2 + n2) * 512 + 512],
                                            lnT2[:, k * 128:(k + 1) * 128],
                                            res[:, k * HID + n2 * 512: k * HID + (n2 + 1) * 512],
                                            start=(k == 0), stop=(k == KH - 1)))
                                c.t(lambda k=k: nc.tensor.matmul(
                                    psP[:, 4 * 512: 4 * 512 + H],
                                    lnT2[:, k * 128:(k + 1) * 128], wb_res[:, k * H:(k + 1) * H],
                                    start=(k == 0), stop=(k == KH - 1)), inc=(k == KH - 1))
                            sw2 = sems.val("te")
                            c.v(lambda par=par: nc.vector.tensor_tensor(sq2_s[:], qt_s[par][:], qt_s[par][:], ALU.mult))
                            c.v(lambda par=par: nc.vector.tensor_reduce(
                                aux_s[par][:, 16:32], sq2_s[:].rearrange("p (h d) -> p h d", h=H),
                                axis=AX.X, op=ALU.add))
                            c.v(lambda par=par: nc.vector.tensor_tensor(sq2_s[:], kt_s[par][:], kt_s[par][:], ALU.mult))
                            auxv = c.v(lambda par=par: nc.vector.tensor_reduce(
                                aux_s[par][:, 32:48], sq2_s[:].rearrange("p (h d) -> p h d", h=H),
                                axis=AX.X, op=ALU.add))
                            qsv = c.v(lambda par=par: nc.vector.tensor_tensor(
                                qst_s[par][:].rearrange("p (h d) -> p h d", h=H),
                                qt_s[par][:].rearrange("p (h d) -> p h d", h=H),
                                a_all[:].broadcast_to([128, H, D]), ALU.mult))
                            for nm, src, dst in (("q", qt_s[par], qT_d), ("k", kt_s[par], kT_d), ("qs", qst_s[par], qsT_d)):
                                rotp = tg_count % 2
                                if c.TE:
                                    c.w("ve", qsv)
                                    if tg_count >= 2:
                                        c.w("ve", tg_evac[tg_count - 2])
                                for f in range(KH):
                                    c.t(lambda f=f, src=src, rotp=rotp: nc.tensor.transpose(
                                        psPT[:, rotp * 1024 + f * 128: rotp * 1024 + (f + 1) * 128],
                                        src[:, f * 128:(f + 1) * 128], ident_sb[:]),
                                        inc=(f == KH - 1))
                                tv = sems.val("te")
                                if c.VE:
                                    c.w("te", tv)
                                    if tg_count >= 2 and (tg_count % 2) in trev_st:
                                        c.w("dma_s", trev_st[tg_count % 2])
                                tev = c.v(lambda rotp=rotp: nc.vector.tensor_copy(
                                    trev[rotp][:], psPT[:, rotp * 1024: rotp * 1024 + 1024]))
                                tg_evac[tg_count] = tev
                                if c.AC:
                                    c.w("ve", tev)
                                trev_st[rotp] = c.dma(c.AC, lambda dst=dst, rotp=rotp: e.dma_start(
                                    out=dst[t].rearrange("f p t2 -> p f t2"),
                                    in_=trev[rotp][:].rearrange("p (f t2) -> p f t2", f=KH)), "dma_s")
                                tg_count += 1
                            if c.AC:
                                c.w("te", sw2)
                            bta = c.a(lambda par=par: nc.scalar.activation(
                                aux_s[par][:, 0:H], psP[:, 4 * 512: 4 * 512 + H], AF.Sigmoid))
                            btas[t] = bta
                            if c.VE:
                                c.w("te", sw2)
                                c.w("ve", bc_done)
                            c.v(lambda: nc.vector.tensor_tensor(
                                sq2_s[:, 0:512], psP[:, 0:512], dtb_bc[:, 0:512], ALU.subtract))
                            fv = c.v(lambda: nc.vector.tensor_tensor(
                                sq2_s[:, 512:1024], psP[:, 512:1024], dtb_bc[:, 512:1024], ALU.subtract))
                            if c.AC:
                                c.w("ve", fv)
                            ga = c.a(lambda par=par: nc.scalar.activation(gt_s[par][:], sq2_s[:], AF.Sigmoid))
                            c.v(lambda: nc.vector.tensor_tensor(
                                sig_s[:, 0:512], psP[:, 1024:1536], bg_bc[:, 0:512], ALU.add))
                            gv = c.v(lambda: nc.vector.tensor_tensor(
                                sig_s[:, 512:1024], psP[:, 1536:2048], bg_bc[:, 512:1024], ALU.add))
                            gvs[t] = gv
                            if c.AC:
                                c.w("ve", gv)
                            g2a = c.a(lambda: nc.scalar.activation(sq2_s[:], sig_s[:], AF.Sigmoid))
                            if c.VE:
                                c.w("ac", g2a)
                            g2v = c.v(lambda par=par: nc.vector.tensor_tensor(g2t_s[par][:], sq2_s[:], nw_bc[:], ALU.mult))
                            if c.SY:
                                c.w("ve", max(qsv, g2v, auxv))
                                c.w("ac", max(bta, ga))
                            for src, dst in ((kt_s[par], kk_d), (vt_s[par], vv_d), (gt_s[par], gg_d), (g2t_s[par], gg2_d)):
                                c.dma(c.SY, lambda src=src, dst=dst: e.dma_start(
                                    out=dst[t], in_=src[:]), "dma_x")
                            sd = c.dma(c.SY, lambda par=par: e.dma_start(out=aux_d[t], in_=aux_s[par][:]), "dma_x")
                            st_done[t] = sd

                    with nc.Block("attnP") as blk:
                        run_phase(blk, emit_P)

                with ExitStack() as ssc:
                    sn = ssc.enter_context
                    cb = {}
                    for nm in ("qT", "qsT", "kT", "kk", "vv", "gg", "gg2"):
                        cb[nm] = [sn(nc.sbuf_tensor(f"cb_{nm}{i}", [128, HID], BF16)) for i in range(2)]
                    cb_aux = [sn(nc.sbuf_tensor(f"cbaux{i}", [128, 64], F32)) for i in range(2)]
                    x1t = [sn(nc.sbuf_tensor(f"x1t{i}", [128, HID], F32)) for i in range(2)]
                    vB = sn(nc.sbuf_tensor("vB", [128, HID], BF16))
                    ksc_t = sn(nc.sbuf_tensor("ksc_t", [128, HID], BF16))
                    smv = sn(nc.sbuf_tensor("smv", [128, 1024], BF16))
                    gq = sn(nc.sbuf_tensor("gq", [128, HID], F32))
                    o3 = sn(nc.sbuf_tensor("o3", [128, HID], F32))
                    sqo = sn(nc.sbuf_tensor("sqo", [128, HID], F32))
                    o5 = [sn(nc.sbuf_tensor(f"o5_{i}", [128, HID], BF16)) for i in range(2)]
                    oTe = sn(nc.sbuf_tensor("oTe", [128, HID], BF16))
                    Sst = [[sn(nc.sbuf_tensor(f"Sst{s}_{i}", [64, H * D], F32)) for i in range(2)]
                           for s in range(2)]
                    Sbf = [sn(nc.sbuf_tensor(f"Sbf{s}", [128, H * D], BF16)) for s in range(2)]
                    sscr = sn(nc.sbuf_tensor("sscr", [64, H * D], F32))
                    r16 = [sn(nc.sbuf_tensor(f"r16_{i}", [128, 16], F32)) for i in range(4)]
                    combo = [sn(nc.sbuf_tensor(f"combo{i}", [128, HID], F32)) for i in range(2)]
                    psS = sn(nc.psum_tensor("psS", [128, 4096], F32))

                    def emit_S(eng, e):
                        c = EC(eng, e)
                        c.barrier()
                        po_done = {}
                        su_done = {}
                        heads_done = {}
                        sbf_done = {}
                        st2_done = {}
                        ov_done = {}
                        otv_done = {}
                        o3r = {}
                        tvs = {}
                        lvs = {}

                        def emit_tail(jt):
                            dsetj = jt % 2
                            parj = jt % 2
                            gcj = (jt % 2) * N_CHUNK + jt // 2
                            base2 = (2048 + dsetj * 1024) * 2
                            if c.TE:
                                c.w("ve", ov_done[jt])
                            for f in range(KH):
                                c.t(lambda f=f, base2=base2, parj=parj: nc.tensor.transpose(
                                    psS.bitcast(BF16)[:, base2 + f * 128: base2 + (f + 1) * 128],
                                    o5[parj][:, f * 128:(f + 1) * 128], ident_sb[:]),
                                    inc=(f == KH - 1))
                            tvs[jt] = sems.val("te")
                            if c.VE:
                                c.w("te", tvs[jt])
                            otv_done[jt] = c.v(lambda base2=base2: nc.vector.tensor_copy(
                                oTe[:], psS.bitcast(BF16)[:, base2: base2 + 1024]))
                            if c.TE:
                                c.w("ve", otv_done[jt])
                            for n2 in range(2):
                                for k in range(KH):
                                    c.t(lambda n2=n2, k=k, dsetj=dsetj: nc.tensor.matmul(
                                        psS[:, 2048 + dsetj * 1024 + n2 * 512: 2048 + dsetj * 1024 + n2 * 512 + 512],
                                        oTe[:, k * 128:(k + 1) * 128],
                                        wo_res[:, k * HID + n2 * 512: k * HID + (n2 + 1) * 512],
                                        start=(k == 0), stop=(k == KH - 1)),
                                        inc=(n2 == 1 and k == KH - 1))
                            wv_ = sems.val("te")
                            if c.VE:
                                c.w("te", wv_)
                                c.w("dma_x", lvs[jt])
                                if jt >= 2:
                                    c.w("dma_s", st2_done[jt - 2])
                            c.v(lambda dsetj=dsetj, parj=parj: nc.vector.tensor_add(
                                combo[parj][:, 0:512],
                                psS[:, 2048 + dsetj * 1024: 2048 + dsetj * 1024 + 512],
                                x1t[parj][:, 0:512]))
                            cv = c.v(lambda dsetj=dsetj, parj=parj: nc.vector.tensor_add(
                                combo[parj][:, 512:1024],
                                psS[:, 2048 + dsetj * 1024 + 512: 2048 + dsetj * 1024 + 1024],
                                x1t[parj][:, 512:1024]))
                            po_done[jt] = cv
                            if c.AC:
                                c.w("ve", cv)
                            st2_done[jt] = c.dma(c.AC, lambda parj=parj, gcj=gcj: e.dma_start(
                                out=x2_d[gcj * 128:(gcj + 1) * 128, :], in_=combo[parj][:]), "dma_s")

                        for it in range(N_TILES):
                            sq = it % 2
                            ch = it // 2
                            gc = sq * N_CHUNK + ch
                            par = it % 2
                            dset = sq
                            if c.SY and it >= 2:
                                c.w("ve", po_done[it - 2])
                            for nm, src in (("qT", qT_d), ("qsT", qsT_d), ("kT", kT_d)):
                                c.dma(c.SY, lambda nm=nm, src=src: e.dma_start(
                                    out=cb[nm][par][:].rearrange("p (f t2) -> p f t2", f=KH),
                                    in_=src[gc].rearrange("f p t2 -> p f t2")), "dma_x")
                            for nm, src in (("kk", kk_d), ("vv", vv_d), ("gg", gg_d), ("gg2", gg2_d)):
                                c.dma(c.SY, lambda nm=nm, src=src: e.dma_start(
                                    out=cb[nm][par][:], in_=src[gc]), "dma_x")
                            c.dma(c.SY, lambda: e.dma_start(out=cb_aux[par][:], in_=aux_d[gc]), "dma_x")
                            lv = c.dma(c.SY, lambda: e.dma_start(
                                out=x1t[par][:], in_=x1_d[gc * 128:(gc + 1) * 128, :]), "dma_x")
                            lvs[it] = lv
                            if c.AC:
                                c.w("dma_x", lv)
                            c.a(lambda: nc.scalar.activation(r16[0][:], cb_aux[par][:, 16:32], AF.Ln, bias=eps24_sb[:]))
                            c.a(lambda: nc.scalar.activation(r16[0][:], r16[0][:], AF.Exp, scale=-0.5))
                            c.a(lambda: nc.scalar.activation(r16[1][:], cb_aux[par][:, 32:48], AF.Ln, bias=eps24_sb[:]))
                            rka = c.a(lambda: nc.scalar.activation(r16[1][:], r16[1][:], AF.Exp, scale=-0.5))
                            if c.VE:
                                c.w("ac", rka)
                                if it >= 1:
                                    c.w("te", heads_done[it - 1])
                            c.v(lambda: nc.vector.tensor_tensor(r16[2][:], r16[1][:], cb_aux[par][:, 0:16], ALU.mult))
                            c.v(lambda: nc.vector.tensor_tensor(
                                vB[:].rearrange("p (h d) -> p h d", h=H),
                                cb["vv"][par][:].rearrange("p (h d) -> p h d", h=H),
                                r16[2][:].broadcast_to([128, H, D]), ALU.mult))
                            c.v(lambda: nc.vector.tensor_tensor(
                                ksc_t[:].rearrange("p (h d) -> p h d", h=H),
                                cb["kk"][par][:].rearrange("p (h d) -> p h d", h=H),
                                kdec_all[:].broadcast_to([128, H, D]), ALU.mult))
                            prep = c.v(lambda: nc.vector.tensor_tensor(
                                gq[:].rearrange("p (h d) -> p h d", h=H),
                                cb["gg"][par][:].rearrange("p (h d) -> p h d", h=H),
                                r16[0][:].broadcast_to([128, H, D]), ALU.mult))
                            if ch == 0:
                                c.v(lambda sq=sq: nc.vector.memset(Sst[sq][0][:], 0.0))
                                prep = c.v(lambda sq=sq: nc.vector.memset(Sbf[sq][:], 0.0))
                            
                            if c.TE:
                                c.w("ve", prep)
                                c.w("dma_x", lv)
                                if it >= 2:
                                    c.w("ac", sbf_done[it - 2])
                                    c.w("ve", po_done[it - 2])
                                    c.w("ve", su_done[it - 2])
                                if it >= 1:
                                    c.w("ve", o3r[it - 1])
                            msk_done = {}
                            for h0 in range(0, H, 2):
                                mm1p = {}
                                for dh in range(2):
                                    h = h0 + dh
                                    fb = h // 2
                                    pb = (h % 2) * 64
                                    if c.TE and h0 >= 2 and dh == 0:
                                        c.w("ve", msk_done[h0 - 1])
                                    mm1p[dh] = c.tmark(lambda h=h, fb=fb, pb=pb, dh=dh: nc.tensor.matmul(
                                        psS[:, 1024 + dh * 512: 1024 + dh * 512 + 128],
                                        cb["kT"][par][pb:pb + 64, fb * 128:(fb + 1) * 128],
                                        cb["qT"][par][pb:pb + 64, fb * 128:(fb + 1) * 128],
                                        start=True, stop=True))
                                if c.VE:
                                    c.w("te", mm1p[1])
                                for dh in range(2):
                                    h = h0 + dh
                                    msk_done[h] = c.v(lambda h=h, dh=dh: nc.vector.tensor_tensor(
                                        smv[:, dh * 128:(dh + 1) * 128],
                                        psS[:, 1024 + dh * 512: 1024 + dh * 512 + 128],
                                        dt_all[:, h * 128:(h + 1) * 128], ALU.mult))
                                for dh in range(2):
                                    h = h0 + dh
                                    fb = h // 2
                                    pb = (h % 2) * 64
                                    if c.TE:
                                        c.w("ve", msk_done[h])
                                    c.t(lambda h=h, dh=dh: nc.tensor.matmul(
                                        psS[:, h * 64:(h + 1) * 64],
                                        smv[:, dh * 128:(dh + 1) * 128],
                                        vB[:, h * 64:(h + 1) * 64], start=True, stop=False))
                                    c.t(lambda h=h, fb=fb, pb=pb, sq=sq: nc.tensor.matmul(
                                        psS[:, h * 64:(h + 1) * 64],
                                        cb["qsT"][par][pb:pb + 64, fb * 128:(fb + 1) * 128],
                                        Sbf[sq][pb:pb + 64, h * 64:(h + 1) * 64], start=False, stop=True))
                                    c.t(lambda h=h, dset=dset: nc.tensor.matmul(
                                        psS[0:64, 2048 + dset * 1024 + h * 64: 2048 + dset * 1024 + (h + 1) * 64],
                                        ksc_t[:, h * 64:(h + 1) * 64],
                                        vB[:, h * 64:(h + 1) * 64], start=True, stop=True),
                                        inc=(h == H - 1))
                            heads_done[it] = sems.val("te")
                            if c.VE:
                                c.w("te", heads_done[it])
                            c.v(lambda sq=sq, ch=ch: nc.vector.tensor_tensor(
                                sscr[:].rearrange("p (h d) -> p h d", h=H),
                                Sst[sq][ch % 2][:].rearrange("p (h d) -> p h d", h=H),
                                a128_all[0:64, :].broadcast_to([64, H, D]), ALU.mult))
                            su = c.v(lambda dset=dset, sq=sq, ch=ch: nc.vector.tensor_add(
                                Sst[sq][(ch + 1) % 2][:],
                                sscr[:],
                                psS[0:64, 2048 + dset * 1024: 2048 + dset * 1024 + 1024]))
                            su_done[it] = su
                            if c.AC:
                                c.w("ve", su)
                            c.a(lambda sq=sq, ch=ch: nc.scalar.copy(Sbf[sq][0:64, :], Sst[sq][(ch + 1) % 2][:]))
                            sbf_done[it] = c.a(lambda sq=sq, ch=ch: nc.scalar.copy(Sbf[sq][64:128, :], Sst[sq][(ch + 1) % 2][:]))
                            o3r[it] = c.v(lambda: nc.vector.tensor_tensor(
                                o3[:], psS[:, 0:1024], gq[:], ALU.mult))
                            c.v(lambda: nc.vector.tensor_tensor(sqo[:], o3[:], o3[:], ALU.mult))
                            msv = c.v(lambda: nc.vector.tensor_reduce(
                                r16[3][:], sqo[:].rearrange("p (h d) -> p h d", h=H),
                                axis=AX.X, op=ALU.add))
                            if c.AC:
                                c.w("ve", msv)
                            c.a(lambda: nc.scalar.activation(r16[3][:], r16[3][:], AF.Ln,
                                                             scale=1.0 / D, bias=eps5_sb[:]))
                            rst = c.a(lambda: nc.scalar.activation(r16[3][:], r16[3][:], AF.Exp, scale=-0.5))
                            if c.VE:
                                c.w("ac", rst)
                            c.v(lambda: nc.vector.tensor_tensor(
                                o3[:].rearrange("p (h d) -> p h d", h=H),
                                o3[:].rearrange("p (h d) -> p h d", h=H),
                                r16[3][:].broadcast_to([128, H, D]), ALU.mult))
                            if c.VE and it >= 2:
                                c.w("te", tvs[it - 2])
                            ov_done[it] = c.v(lambda par=par: nc.vector.tensor_tensor(
                                o5[par][:], o3[:], cb["gg2"][par][:], ALU.mult))
                            if it >= 1:
                                emit_tail(it - 1)
                        emit_tail(N_TILES - 1)

                    with nc.Block("attnS") as blk:
                        run_phase(blk, emit_S)

        if "ffn1" in phases:
            dst1 = yout if tap == "x1" else x1_d
            ffn_sublayer("f1", xin, dst1, stash_stats=True, first_call=True)
        if "attn" in phases:
            attn_sublayer()
        if "ffn2" in phases:
            ffn_sublayer("f2", x2_d, yout, stash_stats=False,
                         first_call=("ffn1" not in phases))
        if tap == "x2" and "attn" in phases:
            def emit_cp(eng, e):
                c = EC(eng, e)
                c.barrier()
                for t in range(N_TILES):
                    c.dma(c.SY, lambda t=t: e.dma_start(
                        out=yout[t * 128:(t + 1) * 128, :],
                        in_=x2_d[t * 128:(t + 1) * 128, :]), "st")
            with nc.Block("cp") as blk:
                run_phase(blk, emit_cp)

        if tap == "dbg1":
            def emit_dbg(eng, e):
                c = EC(eng, e)
                c.barrier()
                c.dma(c.SY, lambda: e.dma_start(out=dbg_w[:, 0:N_TILES], in_=w1s[:]), "st")
                c.dma(c.SY, lambda: e.dma_start(out=dbg_w[:, N_TILES:2 * N_TILES], in_=w2s[:]), "st")
                c.dma(c.SY, lambda: e.dma_start(
                    out=dbg_w.bitcast(U32)[:, 2 * N_TILES:3 * N_TILES], in_=d1u[:]), "st")
                c.dma(c.SY, lambda: e.dma_start(
                    out=dbg_w.bitcast(U32)[:, 3 * N_TILES:4 * N_TILES], in_=d2u[:]), "st")
                for t in range(E * C_LOC // 128):
                    c.dma(c.GP, lambda t=t: e.dma_start(
                        out=dbg_xe[t * 128:(t + 1) * 128, :], in_=xe_d[t * 128:(t + 1) * 128, :]), "st")
                    c.dma(c.GP, lambda t=t: e.dma_start(
                        out=dbg_ye[t * 128:(t + 1) * 128, :], in_=ye_d[t * 128:(t + 1) * 128, :]), "st")
                for t in range(I // 128):
                    c.dma(c.GP, lambda t=t: e.dma_start(
                        out=dbg_dwn[t * 128:(t + 1) * 128, :],
                        in_=down_bf[DBG_EX, t * 128:(t + 1) * 128, :]), "st")
            with nc.Block("dbg") as blk:
                run_phase(blk, emit_dbg)

        def emit_fin(eng, e):
            c = EC(eng, e)
            if c.SY:
                c.barrier()
        with nc.Block("fin") as blk:
            run_phase(blk, emit_fin)

    return nc


_PROG_CACHE = {}


def _get_program(phases, tap):
    key = (tuple(phases), tap)
    if key not in _PROG_CACHE:
        _PROG_CACHE[key] = build_program(phases=tuple(phases), tap=tap)
    return _PROG_CACHE[key]


def kernel(L0_attn_x, L0_attn_Wq, L0_attn_Wk, L0_attn_Wv, L0_attn_Wf, L0_attn_Wb,
           L0_attn_A_log, L0_attn_dt_bias, L0_attn_Wg, L0_attn_bg, L0_attn_Wo,
           L0_attn_norm_w, L0_ffn_router_gate, L0_ffn_gate_w, L0_ffn_up_w,
           L0_ffn_down_w, _phases=("ffn1", "attn", "ffn2"), _tap=None):
    x = np.ascontiguousarray(np.asarray(L0_attn_x, dtype=np.float32))
    full = {
        "Wq": np.ascontiguousarray(np.asarray(L0_attn_Wq, np.float32)),
        "Wk": np.ascontiguousarray(np.asarray(L0_attn_Wk, np.float32)),
        "Wv": np.ascontiguousarray(np.asarray(L0_attn_Wv, np.float32)),
        "Wf": np.ascontiguousarray(np.asarray(L0_attn_Wf, np.float32)),
        "Wb": np.ascontiguousarray(np.asarray(L0_attn_Wb, np.float32)),
        "A_log": np.ascontiguousarray(np.asarray(L0_attn_A_log, np.float32)),
        "dt_bias": np.ascontiguousarray(np.asarray(L0_attn_dt_bias, np.float32)),
        "Wg": np.ascontiguousarray(np.asarray(L0_attn_Wg, np.float32)),
        "bg": np.ascontiguousarray(np.asarray(L0_attn_bg, np.float32)),
        "Wo": np.ascontiguousarray(np.asarray(L0_attn_Wo, np.float32)),
        "norm_w": np.ascontiguousarray(np.asarray(L0_attn_norm_w, np.float32)),
        "router_gate": np.ascontiguousarray(np.asarray(L0_ffn_router_gate, np.float32)),
        "gate_w": np.ascontiguousarray(np.asarray(L0_ffn_gate_w, np.float32)),
        "up_w": np.ascontiguousarray(np.asarray(L0_ffn_up_w, np.float32)),
        "down_w": np.ascontiguousarray(np.asarray(L0_ffn_down_w, np.float32)),
    }
    nc = _get_program(_phases, _tap)
    in_maps = []
    for cix in range(N_CORES):
        m = dict(full)
        m["x_sh"] = np.ascontiguousarray(x[cix * B_CORE:(cix + 1) * B_CORE].reshape(T_LOC, HID))
        in_maps.append(m)
    res = run_bass_kernel_spmd(nc, in_maps, core_ids=list(range(N_CORES)))
    out = np.concatenate([r["y_sh"].reshape(B_CORE, S, HID) for r in res.results], axis=0)
    return out
